# version 03
# baseline (speedup 1.0000x reference)
import numpy as np
from contextlib import ExitStack

import concourse.bass as bass
import concourse.mybir as mybir
from concourse.bass_utils import run_bass_kernel_spmd

F32 = mybir.dt.float32
BF16 = mybir.dt.bfloat16
U8 = mybir.dt.uint8
AF = mybir.ActivationFunctionType
ALU = mybir.AluOpType
AX = mybir.AxisListType
AP = bass.AP

NCORES = 8
D = 1024
LN_EPS = 1e-5


class Plan:
    ENG = ("t", "v", "s", "g", "y")

    def __init__(self, nc, ctx):
        self.nc = nc
        self.ctx = ctx
        self.q = {e: [] for e in self.ENG}
        self.sem = {}
        self.cnt = {}
        for name in ("T", "V", "S", "G", "DM", "DW", "DO"):
            self.new_sem(name)

    def new_sem(self, name):
        self.sem[name] = self.ctx.enter_context(self.nc.semaphore(name))
        self.cnt[name] = 0
        return name

    OWN = {"v": "V", "s": "S", "g": "G", "t": "T"}
    DMASEMS = ("DM", "DW", "DO")

    def op(self, e, fn, inc=None, n=1):
        is_dma = inc is not None and (inc in self.DMASEMS or "_d" in inc)
        if e in ("v", "s", "g") and not is_dma:
            own = self.OWN[e]
            assert inc is None or inc == own, (e, inc)
            self.wait(e, own)
            inc = own
            n = 1
        if inc is not None:
            self.cnt[inc] += n
            sem = self.sem[inc]
            self.q[e].append(lambda eng, fn=fn, sem=sem, n=n: fn(eng).then_inc(sem, n))
        else:
            self.q[e].append(lambda eng, fn=fn: fn(eng))
        return self.cnt[inc] if inc is not None else None

    def wait(self, e, name, val=None):
        v = self.cnt[name] if val is None else val
        sem = self.sem[name]
        self.q[e].append(lambda eng, sem=sem, v=v: eng.wait_ge(sem, v))

    def val(self, name):
        return self.cnt[name]

    def build(self, block):
        def mk(ops):
            def body(eng):
                for f in ops:
                    f(eng)
            return body
        block.tensor(mk(self.q["t"]))
        block.vector(mk(self.q["v"]))
        block.scalar(mk(self.q["s"]))
        block.gpsimd(mk(self.q["g"]))
        block.sync(mk(self.q["y"]))


def _sb_panels(Wt):
    K, N = Wt.shape
    A = Wt.reshape(K // 128, 128, N // 128, 128)
    return np.ascontiguousarray(A.transpose(2, 1, 0, 3).reshape(N // 128, 128, K))


def _scale_cols(s):
    return np.ascontiguousarray(np.asarray(s, np.float32).reshape(-1, 128).T)


def emit_ln(P, x_ap, out_ap, scr, width=D):
    sq, c0, c1, c2, c3 = (scr[k][:] for k in ("sq", "c0", "c1", "c2", "c3"))
    P.wait("s", "V")
    P.op("v", lambda e, x=x_ap: e.tensor_reduce(c0, x, AX.X, ALU.add), inc="V")
    P.op("s", lambda e, x=x_ap: e.activation(sq, x, AF.Square, accum_out=c1), inc="S")
    P.op("v", lambda e: e.tensor_scalar(c0, c0, 1.0 / width, None, ALU.mult), inc="V")
    P.op("v", lambda e: e.tensor_tensor(c2, c0, c0, ALU.mult), inc="V")
    P.wait("v", "S")
    P.op("v", lambda e: e.scalar_tensor_tensor(c3, c1, 1.0 / width, c2, ALU.mult, ALU.subtract), inc="V")
    P.wait("s", "V")
    P.op("s", lambda e: e.activation(c3, c3, AF.Ln, bias=scr["eps"]), inc="S")
    P.op("s", lambda e: e.activation(c3, c3, AF.Exp, scale=-0.5), inc="S")
    P.wait("v", "S")
    P.op("v", lambda e, x=x_ap, o=out_ap: e.tensor_scalar(o, x, c0, c3, ALU.subtract, ALU.mult), inc="V")


class PanelStream:

    def __init__(self, P, nc, ctx, name, width, conv, nslots=2):
        self.P = P
        self.conv = conv
        self.width = width
        src_dt = U8 if conv in ("pm1", "sub") else F32
        self.raw = [ctx.enter_context(nc.sbuf_tensor(f"{name}_r{i}", [128, width], src_dt))
                    for i in range(nslots)]
        self.raw2 = ([ctx.enter_context(nc.sbuf_tensor(f"{name}_q{i}", [128, width], src_dt))
                      for i in range(nslots)] if conv == "sub" else None)
        self.bf = [ctx.enter_context(nc.sbuf_tensor(f"{name}_b{i}", [128, width], BF16))
                   for i in range(nslots)]
        self.last_mm = [0] * nslots
        self.cast_done = [0] * nslots
        self.sems = [P.new_sem(f"{name}_d{i}") for i in range(nslots)]

    def load(self, slot, src_ap, n, src_ap2=None, cast_eng="g"):
        P = self.P
        sm = self.sems[slot]
        P.wait("y", "G", self.cast_done[slot])
        P.op("y", lambda e, o=self.raw[slot][:, 0:n], i=src_ap: e.dma_start(out=o, in_=i), inc=sm, n=16)
        if self.conv == "sub":
            P.op("y", lambda e, o=self.raw2[slot][:, 0:n], i=src_ap2: e.dma_start(out=o, in_=i), inc=sm, n=16)
        dma_done = P.val(sm)
        P.wait(cast_eng, sm, dma_done)
        P.wait(cast_eng, "T", self.last_mm[slot])
        if self.conv == "f32":
            P.op(cast_eng, lambda e, o=self.bf[slot][:, 0:n], i=self.raw[slot][:, 0:n]:
                 e.tensor_copy(o, i), inc="G")
        elif self.conv == "pm1":
            P.op(cast_eng, lambda e, o=self.bf[slot][:, 0:n], i=self.raw[slot][:, 0:n]:
                 e.tensor_scalar(o, i, 2.0, -1.0, ALU.mult, ALU.add), inc="G")
        else:
            P.op(cast_eng, lambda e, o=self.bf[slot][:, 0:n], i=self.raw[slot][:, 0:n],
                 j=self.raw2[slot][:, 0:n]: e.tensor_tensor(o, i, j, ALU.subtract), inc="G")
        self.cast_done[slot] = P.val("G")
        return self.cast_done[slot]

    def used(self, slot):
        self.last_mm[slot] = self.P.val("T")


def emit_fm_stage(P, env, q, QT, rhs, streams, noc, K, evac, sub_w=2048, rhs_wait=None):
    SUB = min(K, sub_w)
    nsub = K // SUB
    FREE = min(QT, 512)
    nth = QT // FREE
    ns = len(streams)
    psb = env["psb"]

    def ps_get(si, oc, th):
        return psb[si * 2 + th] if ns == 2 else psb[(oc % 2) * 2 + th]

    evac_done = env.setdefault("evac_done", {})
    for (strm, get_src) in streams:
        a, b = get_src(0, 0, SUB)
        strm.load(0, a, SUB, src_ap2=b)
    for oc in range(noc):
        for sub in range(nsub):
            idx = oc * nsub + sub
            cur = idx % 2
            if idx + 1 < noc * nsub:
                oc2, sub2 = divmod(idx + 1, nsub)
                k0, k1 = sub2 * SUB, (sub2 + 1) * SUB
                for (strm, get_src) in streams:
                    a, b = get_src(oc2, k0, k1)
                    strm.load((idx + 1) % 2, a, SUB, src_ap2=b)
            gv = max(strm.cast_done[cur] for strm, _ in streams)
            P.wait("t", "G", gv)
            if sub == 0:
                key = 0 if ns == 2 else oc % 2
                vv, ss = evac_done.get(key, (0, 0))
                P.wait("t", "V", vv)
                P.wait("t", "S", ss)
            if oc == 0 and rhs_wait is not None:
                for icg in range(SUB // 128):
                    rhs_wait(sub * (SUB // 128) + icg)
            for th in range(nth):
                for icl in range(SUB // 128):
                    st_flag = (sub == 0 and icl == 0)
                    sp_flag = (sub == nsub - 1 and icl == SUB // 128 - 1)
                    for si in range(ns):
                        P.op("t", lambda e, si=si, th=th, icl=icl, cur=cur, sub=sub,
                             st_flag=st_flag, sp_flag=sp_flag, oc=oc:
                             e.matmul(ps_get(si, oc, th)[:, 0:FREE],
                                      streams[si][0].bf[cur][:, icl * 128:(icl + 1) * 128],
                                      rhs(sub * (SUB // 128) + icl, th),
                                      start=st_flag, stop=sp_flag), inc="T")
            for (strm, _) in streams:
                strm.used(cur)
        tv = P.val("T")
        P.wait("v", "T", tv)
        P.wait("s", "T", tv)
        for th in range(nth):
            evac(oc, th, [ps_get(si, oc, th)[:, 0:FREE] for si in range(ns)])
        key = 0 if ns == 2 else oc % 2
        evac_done[key] = (P.val("V"), P.val("S"))


def emit_load_ln_transpose(P, env, x_h, row0, QT, out_xlt):
    xq, ln1, scr, ident, ps_t = env["xq"], env["ln1"], env["scr"], env["ident"], env["ps_t"]
    for tb in range(QT // 128):
        P.wait("y", "V")
        P.wait("y", "DM")
        P.op("y", lambda e, tb=tb: e.dma_start(
            out=xq[:], in_=x_h[row0 + tb * 128:row0 + (tb + 1) * 128, :]), inc="DM", n=16)
        P.wait("v", "DM")
        P.wait("s", "DM")
        emit_ln(P, xq[:], ln1[:], scr)
        P.wait("t", "V")
        for ic in range(8):
            slot = ps_t[:, (ic % 2) * 128:(ic % 2) * 128 + 128]
            P.wait("t", "S")
            P.wait("t", "V")
            P.op("t", lambda e, s=ln1[:, ic * 128:(ic + 1) * 128], o=slot:
                 e.transpose(o, s, ident[:]), inc="T")
            P.wait("s", "T")
            P.op("s", lambda e, o=out_xlt[:, ic, tb * 128:(tb + 1) * 128], s=slot:
                 e.copy(o, s), inc="S")
        P.wait("v", "S")
        P.wait("v", "T")
    P.wait("t", "S")


def emit_final_tile(P, env, x_h, y_h, row0, out_fm, tb):
    ident, ps_t, scr = env["ident"], env["ps_t"], env["ps_t2_scr"]
    ot, xq, yq = env["out_tm"], env["xq2"], env["yq"]
    for ic in range(8):
        slot = ps_t[:, (ic % 2) * 128:(ic % 2) * 128 + 128]
        P.wait("t", "V")
        P.wait("t", "S")
        P.op("t", lambda e, s=out_fm[:, ic, tb * 128:(tb + 1) * 128], o=slot:
             e.transpose(o, s, ident[:]), inc="T")
        P.wait("v", "T")
        P.op("v", lambda e, o=ot[:, ic * 128:(ic + 1) * 128], s=slot: e.tensor_copy(o, s), inc="V")
    P.wait("y", "V")
    P.wait("y", "DM")
    P.op("y", lambda e: e.dma_start(
        out=xq[:], in_=x_h[row0 + tb * 128:row0 + (tb + 1) * 128, :]), inc="DM", n=16)
    emit_ln(P, ot[:], ot[:], scr)
    P.wait("v", "DM")
    P.wait("v", "DO")
    P.op("v", lambda e: e.tensor_tensor(yq[:], ot[:], xq[:], ALU.add), inc="V")
    P.wait("g", "V")
    P.op("g", lambda e: e.dma_start(out=y_h[row0 + tb * 128:row0 + (tb + 1) * 128, :], in_=yq[:]),
         inc="DO", n=16)


def _mk_env(P, nc, ctx, QT, full=True):
    sb = lambda name, shape, dt=F32: ctx.enter_context(nc.sbuf_tensor(name, shape, dt))
    env = {}
    env["ident"] = sb("identity", [128, 128], F32)
    env["xq"] = sb("xq", [128, D], F32)
    env["xq2"] = sb("xq2", [128, D], F32)
    env["yq"] = sb("yq", [128, D], F32)
    env["ln1"] = sb("ln1", [128, D], F32)
    env["out_tm"] = sb("out_tm", [128, D], F32)
    env["cst"] = sb("cst", [128, 8], F32)
    env["scr"] = {"sq": sb("scr_sq", [128, D], F32),
                  "c0": sb("c0", [128, 1], F32), "c1": sb("c1", [128, 1], F32),
                  "c2": sb("c2", [128, 1], F32), "c3": sb("c3", [128, 1], F32),
                  "eps": env["cst"][:, 0:1]}
    if full:
        env["ps_t2_scr"] = {"sq": sb("scr_sq2", [128, D], F32),
                            "c0": sb("d0", [128, 1], F32), "c1": sb("d1", [128, 1], F32),
                            "c2": sb("d2", [128, 1], F32), "c3": sb("d3", [128, 1], F32),
                            "eps": env["cst"][:, 0:1]}
    env["xlt"] = sb("xlt", [128, 8, QT], BF16)
    env["ps_t"] = ctx.enter_context(nc.psum_tensor("ps_t", [128, 256], F32))
    return env


def build_l0ffn(TOK):
    nc = bass.Bass("TRN2", target_bir_lowering=False)
    x_h = nc.dram_tensor("x", [TOK, D], F32, kind="ExternalInput").ap()
    y_h = nc.dram_tensor("y", [TOK, D], F32, kind="ExternalOutput").ap()
    id_h = nc.dram_tensor("ident", [128, 128], F32, kind="ExternalInput").ap()
    sc_h = nc.dram_tensor("scl", [128, 144], F32, kind="ExternalInput").ap()
    w1_h = nc.dram_tensor("w1p", [32, 128, 1024], F32, kind="ExternalInput").ap()
    w2_h = nc.dram_tensor("w2p", [32, 128, 1024], F32, kind="ExternalInput").ap()
    h1_h = nc.dram_tensor("h1p", [32, 128, 4096], F32, kind="ExternalInput").ap()
    h2_h = nc.dram_tensor("h2p", [32, 128, 4096], F32, kind="ExternalInput").ap()
    w3_h = nc.dram_tensor("w3p", [8, 128, 4096], F32, kind="ExternalInput").ap()
    mid0 = nc.dram_tensor("mid0", [32, 128, TOK], BF16).ap()
    mid1 = nc.dram_tensor("mid1", [32, 128, TOK], BF16).ap()
    mid2 = nc.dram_tensor("mid2", [32, 128, TOK], BF16).ap()

    QT = 1024
    NQ = TOK // QT

    with ExitStack() as ctx:
        P = Plan(nc, ctx)
        sb = lambda name, shape, dt=F32: ctx.enter_context(nc.sbuf_tensor(name, shape, dt))
        env = _mk_env(P, nc, ctx, QT)
        scales = sb("scales", [128, 144], F32)
        rhs_buf = sb("rhsb", [128, 32, QT], BF16)
        out_fm = sb("out_fm", [128, 8, QT], F32)
        stg = sb("stg", [128, 512], BF16)
        ha_scr = sb("ha_scr", [128, 512], F32)
        wA = PanelStream(P, nc, ctx, "wA", 2048, "f32")
        wB = PanelStream(P, nc, ctx, "wB", 2048, "f32")
        env["psb"] = [ctx.enter_context(nc.psum_tensor(f"psb{i}", [128, 512], F32))
                      for i in range(4)]

        block = ctx.enter_context(nc.Block())
        P.op("y", lambda e: e.dma_start(out=env["ident"][:], in_=id_h), inc="DM", n=16)
        P.op("y", lambda e: e.dma_start(out=scales[:], in_=sc_h), inc="DM", n=16)
        for e_ in ("t", "v", "s", "g"):
            P.wait(e_, "DM")
        P.op("v", lambda e: e.memset(env["cst"][:], 0.0), inc="V")
        P.op("v", lambda e: e.memset(env["cst"][:, 0:1], LN_EPS), inc="V")

        s1a = lambda oc: scales[:, oc:oc + 1]
        s2a = lambda oc: scales[:, 32 + oc:32 + oc + 1]
        sh1a = lambda oc: scales[:, 64 + oc:64 + oc + 1]
        sh2a = lambda oc: scales[:, 96 + oc:96 + oc + 1]
        s3a = lambda oc: scales[:, 128 + oc:128 + oc + 1]

        lm_sems = [P.new_sem(f"LM{i}") for i in range(4)]
        lm_tot = [0, 0, 0, 0]

        def load_mid(mid, kc, q):
            P.wait("y", "T")
            P.wait("y", "DO")
            for ic in range(kc):
                g = ic * 4 // kc
                P.op("y", lambda e, ic=ic: e.dma_start(
                    out=rhs_buf[:, ic, 0:QT], in_=mid[ic, :, q * QT:(q + 1) * QT]), inc=f"LM{g}", n=16)
                lm_tot[g] = P.val(f"LM{g}")
            kcl = [kc]

            def rw(ic):
                g = ic * 4 // kcl[0]
                P.wait("t", f"LM{g}", lm_tot[g])
            return rw

        def dma_stage_out(mid, oc, q, th, writer):
            P.wait("g", "V" if writer == "v" else "S")
            P.op("g", lambda e, o=mid[oc, :, q * QT + th * 512:q * QT + (th + 1) * 512],
                 i=stg[:, 0:512]: e.dma_start(out=o, in_=i), inc="DO", n=16)
            P.wait("v", "DO")
            P.wait("s", "DO")

        for q in range(NQ):
            emit_load_ln_transpose(P, env, x_h, q * QT, QT, env["xlt"])
            rhs = lambda ic, th: env["xlt"][:, ic, th * 512:(th + 1) * 512]

            def evac1(oc, th, psl, q=q):
                P.op("s", lambda e, oc=oc: e.activation(ha_scr[:], psl[0][:], AF.Silu, scale=s1a(oc)), inc="S")
                P.wait("v", "S")
                P.op("v", lambda e, oc=oc: e.tensor_scalar(stg[:, 0:512], psl[1][:], s2a(oc), None, ALU.mult), inc="V")
                P.op("v", lambda e: e.tensor_tensor(stg[:, 0:512], ha_scr[:], stg[:, 0:512], ALU.mult), inc="V")
                dma_stage_out(mid0, oc, q, th, "v")

            emit_fm_stage(P, env, q, QT, rhs,
                          [(wA, lambda oc, k0, k1: (w1_h[oc][:, k0:k1], None)),
                           (wB, lambda oc, k0, k1: (w2_h[oc][:, k0:k1], None))],
                          32, 1024, evac1)

        for q in range(NQ):
            rw = load_mid(mid0, 32, q)
            rhs = lambda ic, th: rhs_buf[:, ic, th * 512:(th + 1) * 512]

            def evac2(oc, th, psl, q=q):
                P.op("s", lambda e, oc=oc: e.activation(stg[:, 0:512], psl[0][:], AF.Gelu, scale=sh1a(oc)), inc="S")
                dma_stage_out(mid1, oc, q, th, "s")

            emit_fm_stage(P, env, q, QT, rhs, [(wA, lambda oc, k0, k1: (h1_h[oc][:, k0:k1], None))],
                          32, 4096, evac2, rhs_wait=rw)

        for q in range(NQ):
            rw = load_mid(mid1, 32, q)
            rhs = lambda ic, th: rhs_buf[:, ic, th * 512:(th + 1) * 512]

            def evac3(oc, th, psl, q=q):
                P.op("s", lambda e, oc=oc: e.activation(stg[:, 0:512], psl[0][:], AF.Gelu, scale=sh2a(oc)), inc="S")
                dma_stage_out(mid2, oc, q, th, "s")

            emit_fm_stage(P, env, q, QT, rhs, [(wA, lambda oc, k0, k1: (h2_h[oc][:, k0:k1], None))],
                          32, 4096, evac3, rhs_wait=rw)

        for q in range(NQ):
            rw = load_mid(mid2, 32, q)
            rhs = lambda ic, th: rhs_buf[:, ic, th * 512:(th + 1) * 512]

            def evac4(oc, th, psl, q=q):
                P.op("v", lambda e, oc=oc, th=th: e.tensor_scalar(
                    out_fm[:, oc, th * 512:(th + 1) * 512], psl[0][:], s3a(oc), None, ALU.mult), inc="V")

            emit_fm_stage(P, env, q, QT, rhs, [(wA, lambda oc, k0, k1: (w3_h[oc][:, k0:k1], None))],
                          8, 4096, evac4, rhs_wait=rw)
            P.wait("t", "V")
            for tb in range(QT // 128):
                emit_final_tile(P, env, x_h, y_h, q * QT, out_fm, tb)

        P.wait("g", "DO")
        P.build(block)
    return nc


def build_l1ffn(TOK):
    nc = bass.Bass("TRN2", target_bir_lowering=False)
    x_h = nc.dram_tensor("x", [TOK, D], F32, kind="ExternalInput").ap()
    y_h = nc.dram_tensor("y", [TOK, D], F32, kind="ExternalOutput").ap()
    id_h = nc.dram_tensor("ident", [128, 128], F32, kind="ExternalInput").ap()
    on_h = nc.dram_tensor("ones1", [1, 128], F32, kind="ExternalInput").ap()
    s_h = [nc.dram_tensor(f"s{i}", [1, 1], F32, kind="ExternalInput").ap() for i in range(3)]
    k0a_h = nc.dram_tensor("k0a", [32, 128, 1024], U8, kind="ExternalInput").ap()
    k0b_h = nc.dram_tensor("k0b", [32, 128, 1024], U8, kind="ExternalInput").ap()
    k1_h = nc.dram_tensor("k1", [32, 128, 4096], U8, kind="ExternalInput").ap()
    k2_h = nc.dram_tensor("k2", [8, 128, 4096], U8, kind="ExternalInput").ap()
    mid0 = nc.dram_tensor("mid0", [32, 128, TOK], BF16).ap()
    mid1 = nc.dram_tensor("mid1", [32, 128, TOK], BF16).ap()

    QT = 1024
    NQ = TOK // QT

    with ExitStack() as ctx:
        P = Plan(nc, ctx)
        sb = lambda name, shape, dt=F32: ctx.enter_context(nc.sbuf_tensor(name, shape, dt))
        env = _mk_env(P, nc, ctx, QT)
        ones1 = sb("ones1s", [1, 128], F32)
        ssc = sb("ssc", [1, 4], F32)
        sc2 = sb("sc2", [128, 4], F32)
        rhs_buf = sb("rhsb", [128, 32, QT], BF16)
        out_fm = sb("out_fm", [128, 8, QT], F32)
        stg = sb("stg", [128, 512], BF16)
        wS = PanelStream(P, nc, ctx, "wS", 1024, "sub")
        wP = PanelStream(P, nc, ctx, "wP", 2048, "pm1")
        env["psb"] = [ctx.enter_context(nc.psum_tensor(f"psb{i}", [128, 512], F32))
                      for i in range(4)]

        block = ctx.enter_context(nc.Block())
        P.op("y", lambda e: e.dma_start(out=env["ident"][:], in_=id_h), inc="DM", n=16)
        P.op("y", lambda e: e.dma_start(out=ones1[:], in_=on_h), inc="DM", n=16)
        for i in range(3):
            P.op("y", lambda e, i=i: e.dma_start(out=ssc[:, i:i + 1], in_=s_h[i]), inc="DM", n=16)
        for e_ in ("t", "v", "s", "g"):
            P.wait(e_, "DM")
        P.op("v", lambda e: e.memset(env["cst"][:], 0.0))
        P.op("v", lambda e: e.memset(env["cst"][:, 0:1], LN_EPS))
        for i in range(3):
            P.op("t", lambda e, i=i: e.matmul(env["psb"][0][:, i:i + 1], ones1[:], ssc[:, i:i + 1],
                                              start=True, stop=True), inc="T")
        P.wait("s", "T")
        P.op("s", lambda e: e.activation(sc2[:, 0:3], env["psb"][0][:, 0:3], AF.Copy, scale=2.0))
        s2a = lambda i: sc2[:, i:i + 1]

        def load_mid(mid, kc, q):
            P.wait("y", "T")
            P.wait("y", "DO")
            for ic in range(kc):
                P.op("y", lambda e, ic=ic, q=q, mid=mid: e.dma_start(
                    out=rhs_buf[:, ic, 0:QT], in_=mid[ic, :, q * QT:(q + 1) * QT]), inc="DM", n=16)
            P.wait("t", "DM")

        def dma_stage_out(mid, oc, q, th):
            P.wait("g", "S")
            P.op("g", lambda e, o=mid[oc, :, q * QT + th * 512:q * QT + (th + 1) * 512],
                 i=stg[:, 0:512]: e.dma_start(out=o, in_=i), inc="DO", n=16)
            P.wait("v", "DO")
            P.wait("s", "DO")

        for q in range(NQ):
            emit_load_ln_transpose(P, env, x_h, q * QT, QT, env["xlt"])
            rhs = lambda ic, th: env["xlt"][:, ic, th * 512:(th + 1) * 512]

            def evac1(oc, th, psl, q=q):
                P.op("s", lambda e, oc=oc: e.activation(stg[:, 0:512], psl[0][:], AF.Relu, scale=s2a(0)), inc="S")
                dma_stage_out(mid0, oc, q, th)

            emit_fm_stage(P, env, q, QT, rhs,
                          [(wS, lambda oc, k0, k1: (k0a_h[oc][:, k0:k1], k0b_h[oc][:, k0:k1]))],
                          32, 1024, evac1)

        for q in range(NQ):
            rw = load_mid(mid0, 32, q)
            rhs = lambda ic, th: rhs_buf[:, ic, th * 512:(th + 1) * 512]

            def evac2(oc, th, psl, q=q):
                P.op("s", lambda e, oc=oc: e.activation(stg[:, 0:512], psl[0][:], AF.Relu, scale=s2a(1)), inc="S")
                dma_stage_out(mid1, oc, q, th)

            emit_fm_stage(P, env, q, QT, rhs, [(wP, lambda oc, k0, k1: (k1_h[oc][:, k0:k1], None))],
                          32, 4096, evac2)

        for q in range(NQ):
            rw = load_mid(mid1, 32, q)
            rhs = lambda ic, th: rhs_buf[:, ic, th * 512:(th + 1) * 512]

            def evac3(oc, th, psl, q=q):
                P.op("v", lambda e, oc=oc, th=th: e.tensor_scalar(
                    out_fm[:, oc, th * 512:(th + 1) * 512], psl[0][:], s2a(2), None, ALU.mult), inc="V")

            emit_fm_stage(P, env, q, QT, rhs, [(wP, lambda oc, k0, k1: (k2_h[oc][:, k0:k1], None))],
                          8, 4096, evac3)
            P.wait("t", "V")
            for tb in range(QT // 128):
                emit_final_tile(P, env, x_h, y_h, q * QT, out_fm, tb)

        P.wait("g", "DO")
        P.build(block)
    return nc


def l1ffn_launch(x_full, w):
    TOK = x_full.shape[0] // NCORES
    nc = build_l1ffn(TOK)
    k0 = np.asarray(w["k0"]).astype(np.uint8)
    k1 = np.asarray(w["k1"]).astype(np.uint8)
    k2 = np.asarray(w["k2"]).astype(np.uint8)
    shared = {
        "ident": _IDENT, "ones1": np.ones((1, 128), np.float32),
        "s0": np.asarray(w["s0"], np.float32).reshape(1, 1),
        "s1": np.asarray(w["s1"], np.float32).reshape(1, 1),
        "s2": np.asarray(w["s2"], np.float32).reshape(1, 1),
        "k0a": _sb_panels(k0[:1024]), "k0b": _sb_panels(k0[1024:]),
        "k1": _sb_panels(k1), "k2": _sb_panels(k2),
    }
    in_maps = [dict(shared, x=np.ascontiguousarray(x_full[i * TOK:(i + 1) * TOK]))
               for i in range(NCORES)]
    res = _run(nc, in_maps)
    return np.concatenate([r["y"] for r in res.results], axis=0)


def build_l0attn(NB=2, NT=2048):
    nc = bass.Bass("TRN2", target_bir_lowering=False)
    x_h = nc.dram_tensor("x", [NB * NT, D], F32, kind="ExternalInput").ap()
    y_h = nc.dram_tensor("y", [NB * NT, D], F32, kind="ExternalOutput").ap()
    id_h = nc.dram_tensor("ident", [128, 128], F32, kind="ExternalInput").ap()
    idb_h = nc.dram_tensor("identb", [128, 128], BF16, kind="ExternalInput").ap()
    on_h = nc.dram_tensor("ones1", [1, 128], F32, kind="ExternalInput").ap()
    onb_h = nc.dram_tensor("onesb", [128, 128], BF16, kind="ExternalInput").ap()
    mU_h = nc.dram_tensor("maskU", [128, 128], F32, kind="ExternalInput").ap()
    mL_h = nc.dram_tensor("maskL", [128, 128], F32, kind="ExternalInput").ap()
    mK_h = nc.dram_tensor("maskK", [64, 128], F32, kind="ExternalInput").ap()
    mA_h = nc.dram_tensor("maskA", [128, 64], F32, kind="ExternalInput").ap()
    mQ_h = nc.dram_tensor("maskQ", [64, 64], F32, kind="ExternalInput").ap()
    wq_h = nc.dram_tensor("wq", [8, 128, 1024], F32, kind="ExternalInput").ap()
    wk_h = nc.dram_tensor("wk", [8, 128, 1024], F32, kind="ExternalInput").ap()
    wv_h = nc.dram_tensor("wv", [8, 128, 1024], F32, kind="ExternalInput").ap()
    wo_h = nc.dram_tensor("wo", [8, 128, 1024], F32, kind="ExternalInput").ap()
    wf1_h = nc.dram_tensor("wf1", [1, 128, 1024], F32, kind="ExternalInput").ap()
    wf2_h = nc.dram_tensor("wf2", [128, 1024], F32, kind="ExternalInput").ap()
    wb_h = nc.dram_tensor("wbeta", [128, 64], F32, kind="ExternalInput").ap()
    wg1_h = nc.dram_tensor("wog1", [1, 128, 1024], F32, kind="ExternalInput").ap()
    wg2_h = nc.dram_tensor("wog2", [128, 1024], F32, kind="ExternalInput").ap()
    nw_h = nc.dram_tensor("normw", [1, 1024], F32, kind="ExternalInput").ap()
    odbg_h = nc.dram_tensor("odbg", [128, 1024], F32, kind="ExternalOutput").ap()

    SEG = 128
    CH = 64
    NSEG = NT // SEG

    with ExitStack() as ctx:
        P = Plan(nc, ctx)
        sb = lambda name, shape, dt=F32: ctx.enter_context(nc.sbuf_tensor(name, shape, dt))
        env = _mk_env(P, nc, ctx, SEG, full=False)
        ident_b = sb("identb_s", [128, 128], BF16)
        ones1 = sb("ones1s", [1, 128], F32)
        onesb = sb("onesb_s", [128, 128], BF16)
        masks = sb("masks", [128, 128 * 2 + 64 + 64 + 128], F32)
        mU = masks[:, 0:128]
        mL = masks[:, 128:256]
        mK = masks[0:64, 256:384]
        mA = masks[:, 384:448]
        mQ = masks[0:64, 448:512]
        wq_sb = sb("wq_sb", [128, 8, 1024], BF16)
        wk_sb = sb("wk_sb", [128, 8, 1024], BF16)
        wv_sb = sb("wv_sb", [128, 8, 1024], BF16)
        wo_sb = sb("wo_sb", [128, 8, 1024], BF16)
        wf1_sb = sb("wf1_sb", [128, 1024], BF16)
        wf2_sb = sb("wf2_sb", [128, 1024], BF16)
        wb_sb = sb("wb_sb", [128, 64], BF16)
        wg1_sb = sb("wg1_sb", [128, 1024], BF16)
        wg2_sb = sb("wg2_sb", [128, 1024], BF16)
        nw_sb = sb("nw_sb", [128, 1024], F32)
        wstage = sb("wstage", [128, 1024], F32)
        q_t = sb("q_t", [128, 8, SEG], F32)
        k_t = sb("k_t", [128, 8, SEG], F32)
        kh_t = sb("kh_t", [128, 8, SEG], F32)
        v_t = sb("v_t", [128, 8, SEG], BF16)
        sig_t = sb("sig_t", [128, 8, SEG], F32)
        wrec_t = sb("wrec_t", [128, 8, SEG], F32)
        E_t = sb("E_t", [128, 8, SEG], F32)
        Ei_t = sb("Ei_t", [128, 8, SEG], F32)
        Ep_t = sb("Ep_t", [128, 8, SEG], F32)
        Eip_t = sb("Eip_t", [128, 8, SEG], F32)
        kb_t = sb("kb_t", [128, 8, SEG], F32)
        At_t = sb("At_t", [128, 8, 2 * SEG], BF16)
        Bt_t = sb("Bt_t", [128, 8, 2 * SEG], BF16)
        kE_t = sb("kE_t", [128, 8, SEG], BF16)
        qE_t = sb("qE_t", [128, 8, SEG], BF16)
        f1_t = sb("f1_t", [128, 128], BF16)
        og1_t = sb("og1_t", [128, 128], BF16)
        brow = sb("brow", [1, 1024], F32)
        k2_t = sb("k2_t", [128, 8, SEG], BF16)
        rs_t = sb("rs_t", [128, 128], F32)
        o_tm = sb("o_tm", [128, 1024], F32)
        gate_t = sb("gate_t", [128, 1024], F32)
        yw_t = sb("yw_t", [128, 1024], BF16)
        ywT_t = sb("ywT_t", [128, 8, 128], BF16)
        ao_t = sb("ao_t", [128, 1024], F32)
        zero64 = sb("zero64", [128, 64], F32)
        S_f = [sb(f"S_f{h}", [128, 128], F32) for h in range(8)]
        S_b = [sb(f"S_b{h}", [128, 128], BF16) for h in range(8)]
        def tsk(name, shape, dt=BF16):
            return [sb(f"{name}{p}", shape, dt) for p in range(2)]
        LT_s = tsk("LT_s", [128, 128]); LN_s = tsk("LN_s", [128, 128])
        W_s = tsk("W_s", [128, 128]); LkT_s = tsk("LkT_s", [64, 128])
        Aqa_s = tsk("Aqa_s", [128, 64]); Aqk_s = tsk("Aqk_s", [64, 64])
        M0_s = tsk("M0_s", [128, 128]); M_s = tsk("M_s", [128, 128])
        Atm_s = tsk("Atm_s", [128, 128]); kEtm_s = tsk("kEtm_s", [64, 128])
        Vtm_s = tsk("Vtm_s", [64, 128])
        ps_g = [ctx.enter_context(nc.psum_tensor(f"ps_g{p}", [128, 512], F32)) for p in range(2)]
        ps_c = [ctx.enter_context(nc.psum_tensor(f"ps_c{p}", [128, 512], F32)) for p in range(2)]
        ps_m = [ctx.enter_context(nc.psum_tensor(f"ps_m{p}", [128, 512], F32)) for p in range(2)]
        ps_xs = ctx.enter_context(nc.psum_tensor("ps_xs", [128, 768], BF16))
        ps_x = [ps_xs[:, 0:384], ps_xs[:, 384:768]]
        ps_t = env["ps_t"]

        block = ctx.enter_context(nc.Block())

        P.op("y", lambda e: e.dma_start(out=env["ident"][:], in_=id_h), inc="DM", n=16)
        P.op("y", lambda e: e.dma_start(out=ident_b[:], in_=idb_h), inc="DM", n=16)
        P.op("y", lambda e: e.dma_start(out=ones1[:], in_=on_h), inc="DM", n=16)
        P.op("y", lambda e: e.dma_start(out=onesb[:], in_=onb_h), inc="DM", n=16)
        P.op("y", lambda e: e.dma_start(out=masks[:, 0:128], in_=mU_h), inc="DM", n=16)
        P.op("y", lambda e: e.dma_start(out=masks[:, 128:256], in_=mL_h), inc="DM", n=16)
        P.op("y", lambda e: e.dma_start(out=masks[0:64, 256:384], in_=mK_h), inc="DM", n=16)
        P.op("y", lambda e: e.dma_start(out=masks[:, 384:448], in_=mA_h), inc="DM", n=16)
        P.op("y", lambda e: e.dma_start(out=masks[0:64, 448:512], in_=mQ_h), inc="DM", n=16)
        for e_ in ("t", "v", "s", "g"):
            P.wait(e_, "DM")
        P.op("v", lambda e: e.memset(env["cst"][:], 0.0))
        P.op("v", lambda e: e.memset(env["cst"][:, 0:1], LN_EPS))
        P.op("v", lambda e: e.memset(env["cst"][:, 1:2], 1e-6))
        P.op("v", lambda e: e.memset(zero64[:], 0.0))
        eps6 = env["cst"][:, 1:2]

        def loadw(dst, src_ap, n):
            P.wait("y", "G")
            P.wait("y", "DM")
            P.op("y", lambda e: e.dma_start(out=wstage[:, 0:n], in_=src_ap), inc="DM", n=16)
            P.wait("g", "DM")
            P.op("g", lambda e: e.tensor_copy(dst, wstage[:, 0:n]))

        for i in range(8):
            loadw(wq_sb[:, i, :], wq_h[i], 1024)
            loadw(wk_sb[:, i, :], wk_h[i], 1024)
            loadw(wv_sb[:, i, :], wv_h[i], 1024)
            loadw(wo_sb[:, i, :], wo_h[i], 1024)
        loadw(wf1_sb[:], wf1_h[0], 1024)
        loadw(wf2_sb[:], wf2_h, 1024)
        loadw(wb_sb[:], wb_h, 64)
        loadw(wg1_sb[:], wg1_h[0], 1024)
        loadw(wg2_sb[:], wg2_h, 1024)
        P.wait("y", "G")
        P.wait("y", "DM")
        P.op("y", lambda e: e.dma_start(out=wstage[0:1, 0:1024], in_=nw_h), inc="DM", n=16)
        P.wait("t", "DM")
        P.wait("t", "G")
        for hlf in range(2):
            P.op("t", lambda e, hlf=hlf: e.matmul(ps_g[hlf][:, 0:512], ones1[:],
                 wstage[0:1, hlf * 512:(hlf + 1) * 512], start=True, stop=True), inc="T")
        P.wait("v", "T")
        P.op("v", lambda e: e.tensor_copy(nw_sb[:, 0:512], ps_g[0][:, 0:512]))
        P.op("v", lambda e: e.tensor_copy(nw_sb[:, 512:1024], ps_g[1][:, 0:512]))
        gv_init = P.val("G")
        P.wait("t", "G", gv_init)

        ln_scr = env["scr"]
        xlt = env["xlt"]
        last_o_read = {"v": 0}

        def transpose_to(dst_ap, src_ap, pslot, evac_eng="v", inb=False):
            P.wait("t", "V")
            P.wait("t", "S")
            P.op("t", lambda e: e.transpose(ps_x[pslot][:, 0:128], src_ap, ident_b[:]), inc="T")
            P.wait(evac_eng, "T")
            if evac_eng == "v":
                P.op("v", lambda e: e.tensor_copy(dst_ap, ps_x[pslot][:, 0:128]))
            else:
                P.op("s", lambda e: e.copy(dst_ap, ps_x[pslot][:, 0:128]))

        for b in range(NB):
            P.wait("v", "T")
            for h in range(8):
                P.op("v", lambda e, h=h: e.memset(S_f[h][:], 0.0))
                P.op("v", lambda e, h=h: e.memset(S_b[h][:], 0.0))
            for g in range(NSEG):
                row0 = b * NT + g * SEG
                emit_load_ln_transpose(P, env, x_h, row0, SEG, xlt)

                def proj8(w_sb, out_t, act, out_dt_bf=False):
                    for oc in range(8):
                        par = oc % 2
                        P.wait("t", "V")
                        P.wait("t", "S")
                        for ic in range(8):
                            P.op("t", lambda e, oc=oc, ic=ic, par=par: e.matmul(
                                ps_g[par][:, 0:128], w_sb[:, oc, ic * 128:(ic + 1) * 128],
                                xlt[:, ic, :], start=(ic == 0), stop=(ic == 7)), inc="T")
                        P.wait("s" if act else "v", "T")
                        if act == "silu":
                            P.op("s", lambda e, oc=oc, par=par: e.activation(
                                out_t[:, oc, :], ps_g[par][:, 0:128], AF.Silu))
                        elif act == "sigmoid":
                            P.op("s", lambda e, oc=oc, par=par: e.activation(
                                out_t[:, oc, :], ps_g[par][:, 0:128], AF.Sigmoid))
                        else:
                            P.op("v", lambda e, oc=oc, par=par: e.tensor_copy(
                                out_t[:, oc, :], ps_g[par][:, 0:128]))
                proj8(wq_sb, q_t, "silu")
                proj8(wk_sb, k_t, "silu")
                proj8(wv_sb, v_t, "silu")
                P.wait("t", "V")
                P.wait("t", "S")
                for ic in range(8):
                    P.op("t", lambda e, ic=ic: e.matmul(ps_c[0][:, 0:128],
                         wf1_sb[:, ic * 128:(ic + 1) * 128], xlt[:, ic, :],
                         start=(ic == 0), stop=(ic == 7)), inc="T")
                P.wait("v", "T")
                P.op("v", lambda e: e.tensor_copy(f1_t[:], ps_c[0][:, 0:128]))
                vv = P.val("V")
                P.wait("t", "V", vv)
                for oc in range(8):
                    P.op("t", lambda e, oc=oc: e.matmul(ps_c[oc % 2][:, 128:256],
                         wf2_sb[:, oc * 128:(oc + 1) * 128], f1_t[:],
                         start=True, stop=True), inc="T")
                    P.wait("s", "T")
                    P.op("s", lambda e, oc=oc: e.activation(sig_t[:, oc, :],
                         ps_c[oc % 2][:, 128:256], AF.Sigmoid))
                    if oc < 7:
                        P.wait("t", "S")
                P.wait("t", "V")
                P.wait("t", "S")
                for ic in range(8):
                    P.op("t", lambda e, ic=ic: e.matmul(ps_c[0][:, 256:384],
                         wg1_sb[:, ic * 128:(ic + 1) * 128], xlt[:, ic, :],
                         start=(ic == 0), stop=(ic == 7)), inc="T")
                P.wait("v", "T")
                P.op("v", lambda e: e.tensor_copy(og1_t[:], ps_c[0][:, 256:384]))
                for h in range(8):
                    for ic in range(8):
                        P.op("t", lambda e, h=h, ic=ic: e.matmul(
                            ps_m[h // 4][0:1, (h % 4) * 128:(h % 4) * 128 + 128],
                            wb_sb[:, ic * 8 + h:ic * 8 + h + 1], xlt[:, ic, :],
                            start=(ic == 0), stop=(ic == 7)), inc="T")
                P.wait("s", "T")
                P.op("s", lambda e: e.activation(brow[0:1, 0:512], ps_m[0][0:1, 0:512], AF.Sigmoid))
                P.op("s", lambda e: e.activation(brow[0:1, 512:1024], ps_m[1][0:1, 0:512], AF.Sigmoid))
                P.wait("v", "S")
                P.op("v", lambda e: e.tensor_scalar(brow[:], brow[:], -2.0, None, ALU.mult))
                vv = P.val("V")
                P.wait("t", "V", vv)
                P.wait("t", "S")
                for hlf in range(2):
                    P.op("t", lambda e, hlf=hlf: e.matmul(ps_m[hlf][:, 0:512], ones1[:],
                         brow[0:1, hlf * 512:(hlf + 1) * 512], start=True, stop=True), inc="T")
                P.wait("s", "V")
                P.op("s", lambda e: e.activation(k2_t[:, :, :], k_t[:, :, :], AF.Square))
                P.wait("t", "S")
                for h in range(8):
                    par = h % 2
                    P.op("t", lambda e, h=h, par=par: e.matmul(ps_g[par][:, 128:256], onesb[:],
                         k2_t[:, h, :], start=True, stop=True), inc="T")
                    P.wait("s", "T")
                    P.op("s", lambda e, par=par: e.activation(rs_t[:], ps_g[par][:, 128:256],
                         AF.Ln, bias=eps6))
                    P.op("s", lambda e: e.activation(rs_t[:], rs_t[:], AF.Exp, scale=-0.5))
                    P.wait("v", "S")
                    P.op("v", lambda e, h=h: e.tensor_tensor(kh_t[:, h, :], k_t[:, h, :],
                         rs_t[:], ALU.mult))
                    if h < 7:
                        P.wait("t", "V")
                        P.wait("t", "S")
                P.wait("v", "T")
                for hlf in range(2):
                    P.op("v", lambda e, hlf=hlf: e.tensor_tensor(
                        kb_t[:, 4 * hlf:4 * hlf + 4, :], kh_t[:, 4 * hlf:4 * hlf + 4, :],
                        ps_m[hlf][:, 0:512].rearrange("p (h t) -> p h t", h=4), ALU.mult))
                P.op("v", lambda e: e.reciprocal(wrec_t[:, :, :], sig_t[:, :, :]))
                for h in range(8):
                    for c in range(SEG // CH):
                        sl = (slice(None), h, slice(c * CH, (c + 1) * CH))
                        P.op("v", lambda e, sl=sl: e.tensor_tensor_scan(
                            E_t[sl], sig_t[sl], zero64[:], 1.0, ALU.mult, ALU.add))
                        P.op("v", lambda e, sl=sl: e.tensor_tensor_scan(
                            Ei_t[sl], wrec_t[sl], zero64[:], 1.0, ALU.mult, ALU.add))
                P.op("v", lambda e: e.tensor_tensor(Ep_t[:, :, :], E_t[:, :, :], wrec_t[:, :, :], ALU.mult))
                P.op("v", lambda e: e.tensor_tensor(Eip_t[:, :, :], Ei_t[:, :, :], sig_t[:, :, :], ALU.mult))
                def ilv(dst, ph):
                    t = dst.tensor if hasattr(dst, "tensor") else dst
                    return bass.AP(t, ph, [[2 * SEG * 8 // 8 * 8, 128], [2 * SEG, 8], [2, SEG]])
                def ilv2(handle, ph):
                    return bass.AP(handle, ph, [[8 * 2 * SEG, 128], [2 * SEG, 8], [2, SEG]])
                P.op("v", lambda e: e.tensor_tensor(ilv2(At_t, 0), kb_t[:, :, :], Eip_t[:, :, :], ALU.mult))
                P.op("v", lambda e: e.tensor_tensor(ilv2(At_t, 1), kb_t[:, :, :], Ei_t[:, :, :], ALU.mult))
                P.op("v", lambda e: e.tensor_tensor(ilv2(Bt_t, 0), kh_t[:, :, :], Ep_t[:, :, :], ALU.mult))
                P.op("v", lambda e: e.tensor_tensor(ilv2(Bt_t, 1), kh_t[:, :, :], E_t[:, :, :], ALU.mult))
                P.op("v", lambda e: e.tensor_tensor(kE_t[:, :, :], kh_t[:, :, :], Ei_t[:, :, :], ALU.mult))
                P.op("v", lambda e: e.tensor_tensor(qE_t[:, :, :], q_t[:, :, :], E_t[:, :, :], ALU.mult))
                seg_ready_v = P.val("V")
                seg_ready_s = P.val("S")

                def task_stages(c, h, par):
                    At = At_t[:, h, c * 128:(c + 1) * 128]
                    Bt = Bt_t[:, h, c * 128:(c + 1) * 128]
                    kE = kE_t[:, h, c * CH:(c + 1) * CH]
                    qE = qE_t[:, h, c * CH:(c + 1) * CH]
                    Vc = v_t[:, h, c * CH:(c + 1) * CH]
                    Elast = E_t[:, h, (c + 1) * CH - 1:(c + 1) * CH]
                    stages = []

                    def s_grams():
                        P.wait("t", "V")
                        P.wait("t", "S")
                        P.op("t", lambda e: e.matmul(ps_g[par][:, 0:128], At, Bt, start=True, stop=True), inc="T")
                        P.op("t", lambda e: e.matmul(ps_g[par][:, 128:256], Bt, At, start=True, stop=True), inc="T")
                        P.op("t", lambda e: e.matmul(ps_g[par][0:64, 256:384], kE, Bt, start=True, stop=True), inc="T")
                        P.op("t", lambda e: e.matmul(ps_g[par][:, 384:448], At, qE, start=True, stop=True), inc="T")
                        P.op("t", lambda e: e.matmul(ps_g[par][0:64, 448:512], kE, qE, start=True, stop=True), inc="T")
                        P.op("t", lambda e: e.transpose(ps_x[par][:, 0:128], At, ident_b[:]), inc="T")
                        P.op("t", lambda e: e.transpose(ps_x[par][0:64, 128:256], kE, ident_b[:]), inc="T")
                        P.op("t", lambda e: e.transpose(ps_x[par][0:64, 256:384], Vc, ident_b[:]), inc="T")
                        tg = P.val("T")
                        P.wait("v", "T", tg)
                        P.wait("s", "T", tg)
                        P.op("v", lambda e: e.tensor_tensor(LT_s[par][:], ps_g[par][:, 0:128], mU, ALU.mult))
                        P.op("v", lambda e: e.tensor_tensor(LN_s[par][:], ps_g[par][:, 128:256], mL, ALU.mult))
                        P.op("v", lambda e: e.tensor_tensor(LkT_s[par][:], ps_g[par][0:64, 256:384], mK, ALU.mult))
                        P.op("v", lambda e: e.tensor_tensor(Aqa_s[par][:], ps_g[par][:, 384:448], mA, ALU.mult))
                        P.op("v", lambda e: e.tensor_tensor(Aqk_s[par][:], ps_g[par][0:64, 448:512], mQ, ALU.mult))
                        P.op("s", lambda e: e.copy(Atm_s[par][:], ps_x[par][:, 0:128]))
                        P.op("s", lambda e: e.copy(kEtm_s[par][:], ps_x[par][0:64, 128:256]))
                        P.op("s", lambda e: e.copy(Vtm_s[par][:], ps_x[par][0:64, 256:384]))
                        P.op("v", lambda e: e.tensor_tensor(W_s[par][:], LT_s[par][:], ident_b[:], ALU.add))
                        task_state[par] = (P.val("V"), P.val("S"))
                    stages.append(s_grams)

                    for it in range(6):
                        def s_chain(it=it):
                            wv_, sv_ = task_state[par]
                            P.wait("t", "V", wv_)
                            P.op("t", lambda e: e.matmul(ps_c[par][:, 128:256], LN_s[par][:], LT_s[par][:], start=True, stop=True), inc="T")
                            P.op("t", lambda e: e.matmul(ps_c[par][:, 256:384], LT_s[par][:], LN_s[par][:], start=True, stop=True), inc="T")
                            tt = P.val("T")
                            P.wait("v", "T", tt)
                            P.op("v", lambda e: e.tensor_copy(LT_s[par][:], ps_c[par][:, 128:256]))
                            P.op("v", lambda e: e.tensor_copy(LN_s[par][:], ps_c[par][:, 256:384]))
                            wv_ = P.val("V")
                            P.wait("t", "V", wv_)
                            P.op("t", lambda e: e.matmul(ps_c[par][:, 0:128], LN_s[par][:], W_s[par][:], start=True, stop=True), inc="T")
                            tt = P.val("T")
                            P.wait("v", "T", tt)
                            P.op("v", lambda e: e.tensor_tensor(W_s[par][:], ps_c[par][:, 0:128], W_s[par][:], ALU.add))
                            task_state[par] = (P.val("V"), task_state[par][1])
                        stages.append(s_chain)

                    def s_m0():
                        wv_, sv_ = task_state[par]
                        P.wait("t", "V", wv_)
                        P.wait("t", "S", sv_)
                        P.op("t", lambda e: e.matmul(ps_m[par][:, 0:128], Bt, S_b[h][:], start=True, stop=False), inc="T")
                        P.op("t", lambda e: e.matmul(ps_m[par][:, 0:128], LkT_s[par][:], Vtm_s[par][:], start=False, stop=True), inc="T")
                        tt = P.val("T")
                        P.wait("v", "T", tt)
                        P.op("v", lambda e: e.tensor_copy(M0_s[par][:], ps_m[par][:, 0:128]))
                        task_state[par] = (P.val("V"), sv_)
                    stages.append(s_m0)

                    def s_m():
                        wv_, sv_ = task_state[par]
                        P.wait("t", "V", wv_)
                        P.op("t", lambda e: e.matmul(ps_m[par][:, 128:256], W_s[par][:], M0_s[par][:], start=True, stop=True), inc="T")
                        tt = P.val("T")
                        P.wait("v", "T", tt)
                        P.op("v", lambda e: e.tensor_copy(M_s[par][:], ps_m[par][:, 128:256]))
                        task_state[par] = (P.val("V"), sv_)
                    stages.append(s_m)

                    def s_out():
                        wv_, sv_ = task_state[par]
                        P.wait("t", "V", wv_)
                        P.op("t", lambda e: e.matmul(ps_m[par][0:64, 256:384], qE, S_b[h][:], start=True, stop=False), inc="T")
                        P.op("t", lambda e: e.matmul(ps_m[par][0:64, 256:384], Aqa_s[par][:], M_s[par][:], start=False, stop=False), inc="T")
                        P.op("t", lambda e: e.matmul(ps_m[par][0:64, 256:384], Aqk_s[par][:], Vtm_s[par][:], start=False, stop=True), inc="T")
                        P.op("t", lambda e: e.matmul(ps_m[par][:, 384:512][:, 0:128], Atm_s[par][:], M_s[par][:], start=True, stop=False), inc="T")
                        P.op("t", lambda e: e.matmul(ps_m[par][:, 384:512][:, 0:128], kEtm_s[par][:], Vtm_s[par][:], start=False, stop=True), inc="T")
                        tt = P.val("T")
                        P.wait("v", "T", tt)
                        P.op("v", lambda e: e.tensor_copy(
                            o_tm[c * CH:(c + 1) * CH, h * 128:(h + 1) * 128] if c == 0 else
                            o_tm[64:128, h * 128:(h + 1) * 128], ps_m[par][0:64, 256:384]))
                        P.op("v", lambda e: e.tensor_tensor(S_f[h][:], S_f[h][:], ps_m[par][:, 384:512][:, 0:128], ALU.add))
                        P.op("v", lambda e: e.tensor_scalar(S_f[h][:], S_f[h][:], Elast, None, ALU.mult))
                        P.op("v", lambda e: e.tensor_copy(S_b[h][:], S_f[h][:]))
                    stages.append(s_out)
                    return stages

                task_state = {}
                tasks = [(c, h) for c in range(SEG // CH) for h in range(8)]
                for pi in range(0, len(tasks), 2):
                    (cA, hA) = tasks[pi]
                    (cB, hB) = tasks[pi + 1]
                    stA = task_stages(cA, hA, 0)
                    stB = task_stages(cB, hB, 1)
                    for sA, sB in zip(stA, stB):
                        sA()
                        sB()

                if NB * NT <= 256:
                    P.wait("g", "V")
                    P.wait("g", "DO")
                    P.op("g", lambda e: e.dma_start(out=odbg_h, in_=o_tm[:]), inc="DO", n=16)
                    P.wait("v", "DO")
                P.wait("t", "V")
                P.wait("t", "S")
                for hlf in range(2):
                    P.op("t", lambda e, hlf=hlf: e.matmul(ps_g[hlf][:, 256:384].tensor.ap()[:, hlf * 0:0 + 512] if False else ps_g[hlf][:, 0:512],
                         og1_t[:], wg2_sb[:, hlf * 512:(hlf + 1) * 512], start=True, stop=True), inc="T")
                P.wait("s", "T")
                for hlf in range(2):
                    P.op("s", lambda e, hlf=hlf: e.activation(gate_t[:, hlf * 512:(hlf + 1) * 512],
                         ps_g[hlf][:, 0:512], AF.Sigmoid))
                P.wait("v", "S")
                P.op("v", lambda e: e.tensor_tensor(o_tm[:], o_tm[:], gate_t[:], ALU.mult))
                P.op("v", lambda e: e.tensor_tensor(gate_t[:], o_tm[:], o_tm[:], ALU.mult))
                P.op("v", lambda e: e.tensor_reduce(rs_t[:, 0:8].rearrange("p (h x) -> p h x", x=1),
                     gate_t[:].rearrange("p (h d) -> p h d", h=8), AX.X, ALU.add))
                P.wait("s", "V")
                P.op("s", lambda e: e.activation(rs_t[:, 0:8], rs_t[:, 0:8], AF.Ln, scale=1.0 / 128.0, bias=eps6))
                P.op("s", lambda e: e.activation(rs_t[:, 0:8], rs_t[:, 0:8], AF.Exp, scale=-0.5))
                P.wait("v", "S")
                P.op("v", lambda e: e.tensor_tensor(o_tm[:].rearrange("p (h d) -> p h d", h=8),
                     o_tm[:].rearrange("p (h d) -> p h d", h=8),
                     bass.AP(rs_t, 0, [[128 if False else rs_t.shape[1], 128], [1, 8], [0, 128]]),
                     ALU.mult))
                P.op("v", lambda e: e.tensor_tensor(yw_t[:], o_tm[:], nw_sb[:], ALU.mult))
                for ic in range(8):
                    transpose_to(ywT_t[:, ic, :], yw_t[:, ic * 128:(ic + 1) * 128], ic % 2, "s")
                sv2 = P.val("S")
                P.wait("t", "S", sv2)
                P.wait("t", "V")
                for oc in range(8):
                    for ic in range(8):
                        P.op("t", lambda e, oc=oc, ic=ic: e.matmul(
                             ps_m[oc % 2][:, (oc // 2) * 128:(oc // 2) * 128 + 128],
                             ywT_t[:, ic, :], wo_sb[:, oc, ic * 128:(ic + 1) * 128],
                             start=(ic == 0), stop=(ic == 7)), inc="T")
                P.wait("v", "T")
                for oc in range(8):
                    P.op("v", lambda e, oc=oc: e.tensor_copy(ao_t[:, oc * 128:(oc + 1) * 128],
                         ps_m[oc % 2][:, (oc // 2) * 128:(oc // 2) * 128 + 128]))
                emit_ln(P, ao_t[:], ao_t[:], ln_scr)
                P.wait("y", "V")
                P.wait("y", "DM")
                P.op("y", lambda e, row0=row0: e.dma_start(out=env["xq2"][:], in_=x_h[row0:row0 + 128, :]), inc="DM", n=16)
                P.wait("v", "DM")
                P.wait("v", "DO")
                P.op("v", lambda e: e.tensor_tensor(env["yq"][:], ao_t[:], env["xq2"][:], ALU.add))
                P.wait("g", "V")
                P.op("g", lambda e, row0=row0: e.dma_start(out=y_h[row0:row0 + 128, :], in_=env["yq"][:]), inc="DO", n=16)

        P.wait("g", "DO")
        P.build(block)
    return nc


def l0attn_launch(x_full, w, ncores=NCORES):
    nb, nt, _ = x_full.shape
    nc = build_l0attn(nb // ncores, nt)
    n = 128
    iU = np.arange(128)
    maskU = (iU[None, :] > iU[:, None]).astype(np.float32)
    maskL = (iU[:, None] > iU[None, :]).astype(np.float32)
    j64 = np.arange(64)
    maskK = (iU[None, :] > (2 * j64 + 1)[:, None]).astype(np.float32)
    maskA = ((iU[:, None]) <= (2 * j64 + 1)[None, :]).astype(np.float32)
    maskQ = (j64[None, :] >= j64[:, None]).astype(np.float32)
    import ml_dtypes
    shared = {
        "ident": _IDENT, "identb": _IDENT.astype(ml_dtypes.bfloat16),
        "ones1": np.ones((1, 128), np.float32),
        "onesb": np.ones((128, 128), ml_dtypes.bfloat16),
        "maskU": maskU, "maskL": maskL, "maskK": maskK, "maskA": maskA, "maskQ": maskQ,
        "wq": _sb_panels(np.asarray(w["Wq"], np.float32)),
        "wk": _sb_panels(np.asarray(w["Wk"], np.float32)),
        "wv": _sb_panels(np.asarray(w["Wv"], np.float32)),
        "wo": _sb_panels(np.asarray(w["Wo"], np.float32)),
        "wf1": _sb_panels(np.asarray(w["Wf1"], np.float32)),
        "wf2": np.ascontiguousarray(np.asarray(w["Wf2"], np.float32)),
        "wbeta": np.ascontiguousarray(np.asarray(w["Wbeta"], np.float32).reshape(8, 128, 8)
                                      .transpose(1, 0, 2).reshape(128, 64)),
        "wog1": _sb_panels(np.asarray(w["Wog1"], np.float32)),
        "wog2": np.ascontiguousarray(np.asarray(w["Wog2"], np.float32)),
        "normw": np.ascontiguousarray(np.asarray(w["norm_w"], np.float32).reshape(1, 1024)),
    }
    nbc = nb // ncores
    in_maps = [dict(shared, x=np.ascontiguousarray(x_full[nbc * i:nbc * (i + 1)].reshape(nbc * nt, D)))
               for i in range(ncores)]
    res = run_bass_kernel_spmd(nc, in_maps, core_ids=list(range(ncores)))
    return np.stack([r["y"].reshape(nbc, nt, D) for r in res.results]).reshape(nb, nt, D)


def emit_uhg_norm(P, x_ap, out_ap, scr, eps_add, sq_w=D):
    sq, c0, c1 = scr["sq"], scr["c0"][:], scr["c1"][:]
    P.wait("s", "V")
    P.op("s", lambda e, x=x_ap: e.activation(sq[:], x, AF.Square, accum_out=c1), inc="S")
    P.op("s", lambda e: e.activation(c1, c1, AF.Ln), inc="S")
    P.op("s", lambda e: e.activation(c1, c1, AF.Exp, scale=0.5), inc="S")
    P.wait("v", "S")
    P.op("v", lambda e: e.tensor_scalar(c1, c1, eps_add, None, ALU.add))
    P.op("v", lambda e: e.reciprocal(c1, c1))
    P.op("v", lambda e, x=x_ap, o=out_ap: e.tensor_scalar(o, x, c1, None, ALU.mult))


def emit_uhg_ln(P, x_ap, out_ap, scr, g_bc, b_bc, width=D):
    sq, c0, c1, c2, c3 = (scr[k][:] for k in ("sq", "c0", "c1", "c2", "c3"))
    P.wait("s", "V")
    P.op("v", lambda e, x=x_ap: e.tensor_reduce(c0, x, AX.X, ALU.add))
    P.op("s", lambda e, x=x_ap: e.activation(sq, x, AF.Square, accum_out=c1), inc="S")
    P.op("v", lambda e: e.tensor_tensor(c2, c0, c0, ALU.mult))
    P.wait("v", "S")
    P.op("v", lambda e: e.scalar_tensor_tensor(c3, c2, -1.0 / width, c1, ALU.mult, ALU.add))
    P.op("v", lambda e: e.tensor_scalar(c0, c0, 1.0 / width, None, ALU.mult))
    P.wait("s", "V")
    P.op("s", lambda e: e.activation(c3, c3, AF.Ln, scale=1.0 / (width - 1), bias=scr["eps"]), inc="S")
    P.op("s", lambda e: e.activation(c3, c3, AF.Exp, scale=-0.5), inc="S")
    P.wait("v", "S")
    P.op("v", lambda e, x=x_ap, o=out_ap: e.tensor_scalar(o, x, c0, c3, ALU.subtract, ALU.mult))
    P.op("v", lambda e, o=out_ap: e.tensor_tensor(o, o, g_bc, ALU.mult))
    P.op("v", lambda e, o=out_ap: e.tensor_tensor(o, o, b_bc, ALU.add))


def build_l1attn(BC=256):
    nc = bass.Bass("TRN2", target_bir_lowering=False)
    S = 16
    TOK = S * BC
    x_h = nc.dram_tensor("x", [TOK, D], F32, kind="ExternalInput").ap()
    y_h = nc.dram_tensor("y", [TOK, D], F32, kind="ExternalOutput").ap()
    id_h = nc.dram_tensor("ident", [128, 128], F32, kind="ExternalInput").ap()
    idb_h = nc.dram_tensor("identb", [128, 128], BF16, kind="ExternalInput").ap()
    on_h = nc.dram_tensor("ones1", [1, 128], F32, kind="ExternalInput").ap()
    onb_h = nc.dram_tensor("onesb", [128, 128], BF16, kind="ExternalInput").ap()
    wqt_h = nc.dram_tensor("wqt", [8, 128, 1024], F32, kind="ExternalInput").ap()
    wkt_h = nc.dram_tensor("wkt", [8, 128, 1024], F32, kind="ExternalInput").ap()
    wvt_h = nc.dram_tensor("wvt", [8, 128, 1024], F32, kind="ExternalInput").ap()
    wot_h = nc.dram_tensor("wot", [8, 128, 1024], F32, kind="ExternalInput").ap()
    w1p_h = nc.dram_tensor("w1p", [32, 128, 1024], F32, kind="ExternalInput").ap()
    w2p_h = nc.dram_tensor("w2p", [8, 128, 4096], F32, kind="ExternalInput").ap()
    brows_h = nc.dram_tensor("brows", [9, 1024], F32, kind="ExternalInput").ap()
    b1c_h = nc.dram_tensor("b1c", [128, 32], F32, kind="ExternalInput").ap()

    NSL = BC // 128

    with ExitStack() as ctx:
        P = Plan(nc, ctx)
        sb = lambda name, shape, dt=F32: ctx.enter_context(nc.sbuf_tensor(name, shape, dt))
        env = _mk_env(P, nc, ctx, 128, full=False)
        ident_b = sb("identb_s", [128, 128], BF16)
        ones1 = sb("ones1s", [1, 128], F32)
        onesb = sb("onesb_s", [128, 128], BF16)
        bias_bc = sb("bias_bc", [128, 9, 1024], BF16)
        b1c = sb("b1c_s", [128, 32], F32)
        brow_st = sb("brow_st", [1, 1024], F32)
        wproj = sb("wproj", [128, 8, 1024], BF16)
        wstage = sb("wstage2", [128, 1024], F32)
        srcT = sb("srcT_s", [128, 8, 128], BF16)
        kv_bf = sb("kv_bf", [128, 16, 1024], BF16)
        kn_all = sb("kn_all", [128, 16, 16], F32)
        qn_all = sb("qn_all", [128, 16, 16], F32)
        dots = sb("dots", [128, 16, 16, 16], BF16)
        attn = sb("attn", [128, 16, 16, 16], BF16)
        x1_all = sb("x1_all", [128, 16, 1024], BF16)
        qs = sb("qs", [128, 1024], BF16)
        zs = sb("zs", [128, 1024], F32)
        acc = sb("acc", [128, 1024], F32)
        aT = sb("aT", [128, 8, 128], BF16)
        sm1 = sb("sm1", [128, 256], F32)
        sm2 = sb("sm2", [128, 256], F32)
        smr = sb("smr", [128, 16], F32)
        x1T = sb("x1T", [128, 8, 128], BF16)
        hbuf = sb("hbuf", [128, 32, 128], BF16)
        hsq = sb("hsq", [128, 32, 128], BF16)
        ffm = sb("ffm", [128, 8, 128], BF16)
        wA = PanelStream(P, nc, ctx, "wA1", 1024, "f32")
        psb = [ctx.enter_context(nc.psum_tensor(f"psc{i}", [128, 512], F32)) for i in range(4)]
        env["psb"] = psb
        ps_t = env["ps_t"]
        ps_xb = ctx.enter_context(nc.psum_tensor("ps_xb", [128, 256], BF16))

        block = ctx.enter_context(nc.Block())
        scr = env["scr"]
        P.op("y", lambda e: e.dma_start(out=env["ident"][:], in_=id_h), inc="DM", n=16)
        P.op("y", lambda e: e.dma_start(out=ident_b[:], in_=idb_h), inc="DM", n=16)
        P.op("y", lambda e: e.dma_start(out=ones1[:], in_=on_h), inc="DM", n=16)
        P.op("y", lambda e: e.dma_start(out=onesb[:], in_=onb_h), inc="DM", n=16)
        P.op("y", lambda e: e.dma_start(out=b1c[:], in_=b1c_h), inc="DM", n=16)
        for e_ in ("t", "v", "s", "g"):
            P.wait(e_, "DM")
        P.op("v", lambda e: e.memset(env["cst"][:], 0.0))
        P.op("v", lambda e: e.memset(env["cst"][:, 0:1], LN_EPS))
        P.op("v", lambda e: e.memset(env["cst"][:, 2:3], 1e-38))
        for r in range(9):
            P.wait("y", "DM")
            P.wait("y", "V")
            P.op("y", lambda e, r=r: e.dma_start(out=brow_st[:], in_=brows_h[r:r + 1, :]), inc="DM", n=16)
            P.wait("t", "DM")
            P.wait("t", "V")
            for hlf in range(2):
                P.op("t", lambda e, hlf=hlf: e.matmul(psb[hlf][:, 0:512], ones1[:],
                     brow_st[0:1, hlf * 512:(hlf + 1) * 512], start=True, stop=True), inc="T")
            P.wait("v", "T")
            for hlf in range(2):
                P.op("v", lambda e, r=r, hlf=hlf: e.tensor_copy(
                    bias_bc[:, r, hlf * 512:(hlf + 1) * 512], psb[hlf][:, 0:512]))
        BQ, BK, BV, BO, B2, G1, BE1, G2, BE2 = range(9)

        def loadw_phase(w_h):
            for i in range(8):
                P.wait("y", "G")
                P.wait("y", "DM")
                P.op("y", lambda e, i=i: e.dma_start(out=wstage[:], in_=w_h[i]), inc="DM", n=16)
                P.wait("g", "DM")
                P.wait("g", "T")
                P.op("g", lambda e, i=i: e.tensor_copy(wproj[:, i, :], wstage[:]))
            P.wait("t", "G")

        def ln_src(i, sl, out_ap):
            P.wait("y", "V")
            P.wait("y", "DM")
            row = i * BC + sl * 128
            P.op("y", lambda e, row=row: e.dma_start(out=env["xq"][:], in_=x_h[row:row + 128, :]), inc="DM", n=16)
            P.wait("v", "DM")
            P.wait("s", "DM")
            emit_ln(P, env["xq"][:], out_ap, scr)

        def mk_srcT(i, sl):
            ln_src(i, sl, zs[:])
            P.wait("t", "V")
            for ic in range(8):
                P.wait("t", "S")
                P.op("t", lambda e, ic=ic: e.transpose(
                    ps_t[:, (ic % 2) * 128:(ic % 2) * 128 + 128],
                    zs[:, ic * 128:(ic + 1) * 128], env["ident"][:]), inc="T")
                P.wait("s", "T")
                P.op("s", lambda e, ic=ic: e.copy(
                    srcT[:, ic, :], ps_t[:, (ic % 2) * 128:(ic % 2) * 128 + 128]), inc="S")
            P.wait("v", "S")
            P.wait("t", "S")

        def proj_tm(srcT_i, par, oc_half):
            for ic in range(8):
                P.op("t", lambda e, ic=ic, par=par, oc_half=oc_half, srcT_i=srcT_i: e.matmul(
                    psb[par][:, 0:512], srcT_i[:, ic, :],
                    wproj[:, ic, oc_half * 512:(oc_half + 1) * 512],
                    start=(ic == 0), stop=(ic == 7)), inc="T")

        def uhg_project(srcT_i, bias_row, out_ap):
            P.wait("t", "V")
            P.wait("t", "S")
            proj_tm(srcT_i, 0, 0)
            proj_tm(srcT_i, 1, 1)
            tv = P.val("T")
            P.wait("v", "T", tv)
            for hlf in range(2):
                P.op("v", lambda e, hlf=hlf, bias_row=bias_row: e.tensor_tensor(
                    zs[:, hlf * 512:(hlf + 1) * 512], psb[hlf][:, 0:512],
                    bias_bc[:, bias_row, hlf * 512:(hlf + 1) * 512], ALU.add))
            emit_uhg_norm(P, zs[:], out_ap, scr, 1e-8)

        for sl in range(NSL):
            loadw_phase(wkt_h)
            for i in range(16):
                mk_srcT(i, sl)
                uhg_project(srcT[:, :, :], BK, kv_bf[:, i, :])
                P.wait("s", "V")
                P.op("s", lambda e, i=i: e.activation(qs[:], kv_bf[:, i, :], AF.Square))
                P.wait("v", "S")
                P.op("v", lambda e, i=i: e.tensor_reduce(
                    kn_all[:, i, :].rearrange("p h -> p h ()"),
                    qs[:].rearrange("p (h d) -> p h d", h=16), AX.X, ALU.add))
            loadw_phase(wqt_h)
            for i in range(16):
                mk_srcT(i, sl)
                uhg_project(srcT[:, :, :], BQ, qs[:])
                P.wait("s", "V")
                P.op("s", lambda e: e.activation(zs[:], qs[:], AF.Square))
                P.wait("v", "S")
                P.op("v", lambda e, i=i: e.tensor_reduce(
                    qn_all[:, i, :].rearrange("p h -> p h ()"),
                    zs[:].rearrange("p (h d) -> p h d", h=16), AX.X, ALU.add))
                for j in range(16):
                    P.op("v", lambda e, j=j: e.tensor_tensor(zs[:], qs[:], kv_bf[:, j, :], ALU.mult))
                    P.op("v", lambda e, i=i, j=j: e.tensor_reduce(
                        bass.AP(dots, i * 256 + j, [[4096, 128], [16, 16], [1, 1]]),
                        zs[:].rearrange("p (h d) -> p h d", h=16), AX.X, ALU.add))
            loadw_phase(wvt_h)
            for i in range(16):
                mk_srcT(i, sl)
                uhg_project(srcT[:, :, :], BV, kv_bf[:, i, :])
            for i in range(16):
                P.op("v", lambda e, i=i: e.tensor_tensor(
                    sm1[:], bass.AP(qn_all, i * 16, [[256, 128], [1, 16], [0, 16]]),
                    bass.AP(kn_all, 0, [[256, 128], [1, 16], [16, 16]]), ALU.mult))
                P.op("v", lambda e: e.tensor_scalar(sm1[:], sm1[:], 1e-8, None, ALU.add))
                P.op("v", lambda e: e.reciprocal(sm1[:], sm1[:]))
                P.op("v", lambda e, i=i: e.tensor_tensor(sm2[:], dots[:, i, :, :], dots[:, i, :, :], ALU.mult))
                P.op("v", lambda e: e.tensor_tensor(sm2[:], sm2[:], sm1[:], ALU.mult))
                P.op("v", lambda e: e.tensor_scalar(sm2[:], sm2[:], -1.0, 1.0, ALU.mult, ALU.add))
                P.wait("s", "V")
                P.op("s", lambda e: e.activation(sm2[:], sm2[:], AF.Ln, bias=env["cst"][:, 2:3]), inc="S")
                P.op("s", lambda e: e.activation(sm2[:], sm2[:], AF.Exp, scale=0.5), inc="S")
                P.wait("v", "S")
                P.op("v", lambda e: e.tensor_scalar(sm2[:], sm2[:], -1.0, None, ALU.mult))
                P.op("v", lambda e: e.tensor_reduce(smr[:].rearrange("p h -> p h ()"),
                     sm2[:].rearrange("p (h j) -> p h j", h=16), AX.X, ALU.max))
                P.op("v", lambda e: e.tensor_tensor(sm2[:], sm2[:],
                     bass.AP(smr, 0, [[16, 128], [1, 16], [0, 16]]), ALU.subtract))
                P.wait("s", "V")
                P.op("s", lambda e: e.activation(sm2[:], sm2[:], AF.Exp), inc="S")
                P.wait("v", "S")
                P.op("v", lambda e: e.tensor_reduce(smr[:].rearrange("p h -> p h ()"),
                     sm2[:].rearrange("p (h j) -> p h j", h=16), AX.X, ALU.add))
                P.op("v", lambda e: e.reciprocal(smr[:], smr[:]))
                P.op("v", lambda e, i=i: e.tensor_tensor(attn[:, i, :, :], sm2[:].rearrange("p (h j) -> p h j", h=16),
                     bass.AP(smr, 0, [[16, 128], [1, 16], [0, 16]]), ALU.mult))
            loadw_phase(wot_h)
            for i in range(16):
                P.wait("v", "G")
                P.op("v", lambda e: e.memset(acc[:], 0.0))
                for j in range(16):
                    P.op("v", lambda e, i=i, j=j: e.tensor_tensor(
                        zs[:], kv_bf[:, j, :],
                        bass.AP(attn, i * 256 + j, [[4096, 128], [16, 16], [0, 64]]), ALU.mult))
                    P.op("v", lambda e: e.tensor_tensor(acc[:], acc[:], zs[:], ALU.add))
                P.wait("t", "V")
                for ic in range(8):
                    P.wait("t", "S")
                    P.op("t", lambda e, ic=ic: e.transpose(
                        ps_t[:, (ic % 2) * 128:(ic % 2) * 128 + 128],
                        acc[:, ic * 128:(ic + 1) * 128], env["ident"][:]), inc="T")
                    P.wait("s", "T")
                    P.op("s", lambda e, ic=ic: e.copy(aT[:, ic, :],
                         ps_t[:, (ic % 2) * 128:(ic % 2) * 128 + 128]), inc="S")
                sv = P.val("S")
                P.wait("t", "S", sv)
                P.wait("t", "V")
                proj_tm(aT, 0, 0)
                proj_tm(aT, 1, 1)
                tv = P.val("T")
                P.wait("v", "T", tv)
                for hlf in range(2):
                    P.op("v", lambda e, hlf=hlf: e.tensor_tensor(
                        zs[:, hlf * 512:(hlf + 1) * 512], psb[hlf][:, 0:512],
                        bias_bc[:, BO, hlf * 512:(hlf + 1) * 512], ALU.add))
                emit_uhg_norm(P, zs[:], zs[:], scr, 1e-8)
                ln_src(i, sl, env["ln1"][:])
                P.op("v", lambda e: e.tensor_tensor(zs[:], zs[:], env["ln1"][:], ALU.add))
                emit_uhg_ln(P, zs[:], zs[:], scr, bias_bc[:, G1, :], bias_bc[:, BE1, :])
                P.op("v", lambda e, i=i: e.tensor_copy(x1_all[:, i, :], zs[:]))
            for i in range(16):
                P.wait("t", "V")
                for ic in range(8):
                    P.wait("t", "S")
                    P.op("t", lambda e, i=i, ic=ic: e.transpose(
                        ps_xb[:, (ic % 2) * 128:(ic % 2) * 128 + 128],
                        x1_all[:, i, ic * 128:(ic + 1) * 128], ident_b[:]), inc="T")
                    P.wait("s", "T")
                    P.op("s", lambda e, ic=ic: e.copy(
                        x1T[:, ic, 0:128], ps_xb[:, (ic % 2) * 128:(ic % 2) * 128 + 128]), inc="S")
                sv = P.val("S")
                P.wait("t", "S", sv)
                def rhs1(ic, th):
                    return x1T[:, ic, 0:128]
                def evacw1(oc, th, psl):
                    P.wait("v", "S")
                    P.op("v", lambda e, oc=oc: e.tensor_scalar(sm1[:, 0:128], psl[0][:],
                         b1c[:, oc:oc + 1], None, ALU.add))
                    P.wait("s", "V")
                    P.op("s", lambda e, oc=oc: e.activation(hbuf[:, oc, :], sm1[:, 0:128], AF.Relu), inc="S")
                    P.op("s", lambda e, oc=oc: e.activation(hsq[:, oc, :], sm1[:, 0:128], AF.Square), inc="S")
                emit_fm_stage(P, env, 0, 128, rhs1, [(wA, lambda oc, k0, k1: (w1p_h[oc][:, k0:k1], None))],
                              32, 1024, evacw1, sub_w=1024)
                P.wait("t", "S")
                P.wait("t", "V")
                for oc in range(32):
                    P.op("t", lambda e, oc=oc: e.matmul(psb[1][:, 0:128], onesb[:], hsq[:, oc, :],
                         start=(oc == 0), stop=(oc == 31)), inc="T")
                P.wait("s", "T")
                P.op("s", lambda e: e.activation(sm2[:, 0:128], psb[1][:, 0:128], AF.Ln), inc="S")
                P.op("s", lambda e: e.activation(sm2[:, 0:128], sm2[:, 0:128], AF.Exp, scale=0.5), inc="S")
                P.wait("v", "S")
                P.op("v", lambda e: e.tensor_scalar(sm2[:, 0:128], sm2[:, 0:128], 1e-8, None, ALU.add))
                P.op("v", lambda e: e.reciprocal(sm2[:, 0:128], sm2[:, 0:128]))
                for oc in range(32):
                    P.op("v", lambda e, oc=oc: e.tensor_tensor(hbuf[:, oc, :], hbuf[:, oc, :],
                         sm2[:, 0:128], ALU.mult))
                P.wait("t", "S")
                P.wait("t", "V")
                def rhs2(ic, th):
                    return hbuf[:, ic, 0:128]
                def evacw2(oc, th, psl):
                    P.op("v", lambda e, oc=oc: e.tensor_copy(ffm[:, oc, 0:128], psl[0][:]))
                emit_fm_stage(P, env, 0, 128, rhs2, [(wA, lambda oc, k0, k1: (w2p_h[oc][:, k0:k1], None))],
                              8, 4096, evacw2, sub_w=1024)
                P.wait("t", "V")
                for ic in range(8):
                    P.wait("t", "S")
                    P.wait("t", "V")
                    P.op("t", lambda e, ic=ic: e.transpose(
                        ps_xb[:, (ic % 2) * 128:(ic % 2) * 128 + 128],
                        ffm[:, ic, 0:128], ident_b[:]), inc="T")
                    P.wait("v", "T")
                    P.op("v", lambda e, ic=ic: e.tensor_copy(
                        zs[:, ic * 128:(ic + 1) * 128],
                        ps_xb[:, (ic % 2) * 128:(ic % 2) * 128 + 128]))
                P.op("v", lambda e: e.tensor_tensor(zs[:], zs[:], bias_bc[:, B2, :], ALU.add))
                emit_uhg_norm(P, zs[:], zs[:], scr, 1e-8)
                P.op("v", lambda e, i=i: e.tensor_tensor(zs[:], zs[:], x1_all[:, i, :], ALU.add))
                emit_uhg_ln(P, zs[:], zs[:], scr, bias_bc[:, G2, :], bias_bc[:, BE2, :])
                emit_uhg_norm(P, zs[:], zs[:], scr, 0.0)
                emit_ln(P, zs[:], zs[:], scr)
                P.wait("y", "V")
                P.wait("y", "DM")
                row = i * BC + sl * 128
                P.op("y", lambda e, row=row: e.dma_start(out=env["xq2"][:], in_=x_h[row:row + 128, :]), inc="DM", n=16)
                P.wait("v", "DM")
                P.wait("v", "DO")
                P.op("v", lambda e: e.tensor_tensor(env["yq"][:], zs[:], env["xq2"][:], ALU.add))
                P.wait("g", "V")
                P.op("g", lambda e, row=row: e.dma_start(out=y_h[row:row + 128, :], in_=env["yq"][:]), inc="DO", n=16)
        P.wait("g", "DO")
        with nc.allow_low_precision(reason="bf16 dots/attn tiles; DVE accumulates fp32 internally"):
            P.build(block)
    return nc


def l1attn_launch(x_full, w):
    S, NT, _ = x_full.shape
    BC = NT // NCORES
    nc = build_l1attn(BC)
    import ml_dtypes
    brows = np.stack([np.asarray(w[k], np.float32) for k in
                      ("bq", "bk", "bv", "bo", "b2", "g1", "beta1", "g2", "beta2")])
    shared = {
        "ident": _IDENT, "identb": _IDENT.astype(ml_dtypes.bfloat16),
        "ones1": np.ones((1, 128), np.float32),
        "onesb": np.ones((128, 128), ml_dtypes.bfloat16),
        "wqt": np.ascontiguousarray(np.asarray(w["wq"], np.float32).T).reshape(8, 128, 1024),
        "wkt": np.ascontiguousarray(np.asarray(w["wk"], np.float32).T).reshape(8, 128, 1024),
        "wvt": np.ascontiguousarray(np.asarray(w["wv"], np.float32).T).reshape(8, 128, 1024),
        "wot": np.ascontiguousarray(np.asarray(w["wo"], np.float32).T).reshape(8, 128, 1024),
        "w1p": _sb_panels(np.ascontiguousarray(np.asarray(w["w1"], np.float32).T)),
        "w2p": _sb_panels(np.ascontiguousarray(np.asarray(w["w2"], np.float32).T)),
        "brows": brows, "b1c": _scale_cols(np.asarray(w["b1"], np.float32)),
    }
    in_maps = [dict(shared, x=np.ascontiguousarray(x_full[:, i * BC:(i + 1) * BC, :].reshape(S * BC, D)))
               for i in range(NCORES)]
    res = _run(nc, in_maps)
    return np.concatenate([r["y"].reshape(S, BC, D) for r in res.results], axis=1)


_IDENT = np.ascontiguousarray(np.eye(128, dtype=np.float32))


def _run(nc, in_maps):
    return run_bass_kernel_spmd(nc, in_maps, core_ids=list(range(NCORES)))


def l0ffn_launch(x_full, w):
    TOK = x_full.shape[0] // NCORES
    nc = build_l0ffn(TOK)
    scl = np.zeros((128, 144), np.float32)
    scl[:, 0:32] = _scale_cols(w["w1_s"])
    scl[:, 32:64] = _scale_cols(w["w2_s"])
    scl[:, 64:96] = _scale_cols(w["h1_s"])
    scl[:, 96:128] = _scale_cols(w["h2_s"])
    scl[:, 128:136] = _scale_cols(w["w3_s"])
    shared = {
        "ident": _IDENT, "scl": scl,
        "w1p": _sb_panels(np.ascontiguousarray(w["w1_w"].T)),
        "w2p": _sb_panels(np.ascontiguousarray(w["w2_w"].T)),
        "h1p": _sb_panels(np.ascontiguousarray(w["h1_w"].T)),
        "h2p": _sb_panels(np.ascontiguousarray(w["h2_w"].T)),
        "w3p": _sb_panels(np.ascontiguousarray(w["w3_w"].T)),
    }
    in_maps = [dict(shared, x=np.ascontiguousarray(x_full[i * TOK:(i + 1) * TOK]))
               for i in range(NCORES)]
    res = _run(nc, in_maps)
    return np.concatenate([r["y"] for r in res.results], axis=0)


def kernel(**inputs):
    x = np.asarray(inputs["L0_attn_x"], np.float32)
    B, N, _ = x.shape
    w0a = {k[len("L0_attn_"):]: np.asarray(v) for k, v in inputs.items() if k.startswith("L0_attn_") and k != "L0_attn_x"}
    x1 = l0attn_launch(x, w0a)
    w0f = {k[len("L0_ffn_"):]: np.asarray(v) for k, v in inputs.items() if k.startswith("L0_ffn_")}
    x2 = l0ffn_launch(np.ascontiguousarray(x1.reshape(B * N, D)), w0f).reshape(B, N, D)
    w1a = {k[len("L1_attn_"):]: np.asarray(v) for k, v in inputs.items() if k.startswith("L1_attn_")}
    x3 = l1attn_launch(x2, w1a)
    w1f = {k[len("L1_ffn_"):]: np.asarray(v) for k, v in inputs.items() if k.startswith("L1_ffn_")}
    out = l1ffn_launch(np.ascontiguousarray(x3.reshape(B * N, D)), w1f).reshape(B, N, D)
    return out


# version 04
# speedup vs baseline: 1.1742x; 1.1742x over previous
import numpy as np
from contextlib import ExitStack

import concourse.bass as bass
import concourse.mybir as mybir
from concourse.bass_utils import run_bass_kernel_spmd

F32 = mybir.dt.float32
BF16 = mybir.dt.bfloat16
U8 = mybir.dt.uint8
AF = mybir.ActivationFunctionType
ALU = mybir.AluOpType
AX = mybir.AxisListType
AP = bass.AP

NCORES = 8
D = 1024
LN_EPS = 1e-5


class Plan:
    ENG = ("t", "v", "s", "g", "y")

    def __init__(self, nc, ctx):
        self.nc = nc
        self.ctx = ctx
        self.q = {e: [] for e in self.ENG}
        self.sem = {}
        self.cnt = {}
        for name in ("T", "V", "S", "G", "DM", "DW", "DO"):
            self.new_sem(name)

    def new_sem(self, name):
        self.sem[name] = self.ctx.enter_context(self.nc.semaphore(name))
        self.cnt[name] = 0
        return name

    OWN = {"v": "V", "s": "S", "g": "G", "t": "T"}
    DMASEMS = ("DM", "DW", "DO")

    def op(self, e, fn, inc=None, n=1):
        is_dma = inc is not None and (inc in self.DMASEMS or "_d" in inc)
        if e in ("v", "s", "g") and not is_dma:
            own = self.OWN[e]
            assert inc is None or inc == own, (e, inc)
            self.wait(e, own)
            inc = own
            n = 1
        if inc is not None:
            self.cnt[inc] += n
            sem = self.sem[inc]
            self.q[e].append(lambda eng, fn=fn, sem=sem, n=n: fn(eng).then_inc(sem, n))
        else:
            self.q[e].append(lambda eng, fn=fn: fn(eng))
        return self.cnt[inc] if inc is not None else None

    def wait(self, e, name, val=None):
        v = self.cnt[name] if val is None else val
        sem = self.sem[name]
        self.q[e].append(lambda eng, sem=sem, v=v: eng.wait_ge(sem, v))

    def val(self, name):
        return self.cnt[name]

    def build(self, block):
        def mk(ops):
            def body(eng):
                for f in ops:
                    f(eng)
            return body
        block.tensor(mk(self.q["t"]))
        block.vector(mk(self.q["v"]))
        block.scalar(mk(self.q["s"]))
        block.gpsimd(mk(self.q["g"]))
        block.sync(mk(self.q["y"]))


def _sb_panels(Wt):
    K, N = Wt.shape
    A = Wt.reshape(K // 128, 128, N // 128, 128)
    return np.ascontiguousarray(A.transpose(2, 1, 0, 3).reshape(N // 128, 128, K))


def _scale_cols(s):
    return np.ascontiguousarray(np.asarray(s, np.float32).reshape(-1, 128).T)


def emit_ln(P, x_ap, out_ap, scr, width=D):
    sq, c0, c1, c2, c3 = (scr[k][:] for k in ("sq", "c0", "c1", "c2", "c3"))
    P.wait("s", "V")
    P.op("v", lambda e, x=x_ap: e.tensor_reduce(c0, x, AX.X, ALU.add), inc="V")
    P.op("s", lambda e, x=x_ap: e.activation(sq, x, AF.Square, accum_out=c1), inc="S")
    P.op("v", lambda e: e.tensor_scalar(c0, c0, 1.0 / width, None, ALU.mult), inc="V")
    P.op("v", lambda e: e.tensor_tensor(c2, c0, c0, ALU.mult), inc="V")
    P.wait("v", "S")
    P.op("v", lambda e: e.scalar_tensor_tensor(c3, c1, 1.0 / width, c2, ALU.mult, ALU.subtract), inc="V")
    P.wait("s", "V")
    P.op("s", lambda e: e.activation(c3, c3, AF.Ln, bias=scr["eps"]), inc="S")
    P.op("s", lambda e: e.activation(c3, c3, AF.Exp, scale=-0.5), inc="S")
    P.wait("v", "S")
    P.op("v", lambda e, x=x_ap, o=out_ap: e.tensor_scalar(o, x, c0, c3, ALU.subtract, ALU.mult), inc="V")


class PanelStream:

    def __init__(self, P, nc, ctx, name, width, conv, nslots=2):
        self.P = P
        self.conv = conv
        self.width = width
        src_dt = U8 if conv in ("pm1", "sub") else F32
        self.raw = [ctx.enter_context(nc.sbuf_tensor(f"{name}_r{i}", [128, width], src_dt))
                    for i in range(nslots)]
        self.raw2 = ([ctx.enter_context(nc.sbuf_tensor(f"{name}_q{i}", [128, width], src_dt))
                      for i in range(nslots)] if conv == "sub" else None)
        self.bf = [ctx.enter_context(nc.sbuf_tensor(f"{name}_b{i}", [128, width], BF16))
                   for i in range(nslots)]
        self.last_mm = [0] * nslots
        self.cast_done = [0] * nslots
        self.sems = [P.new_sem(f"{name}_d{i}") for i in range(nslots)]

    def load(self, slot, src_ap, n, src_ap2=None, cast_eng="g"):
        P = self.P
        sm = self.sems[slot]
        P.wait("y", "G", self.cast_done[slot])
        P.op("y", lambda e, o=self.raw[slot][:, 0:n], i=src_ap: e.dma_start(out=o, in_=i), inc=sm, n=16)
        if self.conv == "sub":
            P.op("y", lambda e, o=self.raw2[slot][:, 0:n], i=src_ap2: e.dma_start(out=o, in_=i), inc=sm, n=16)
        dma_done = P.val(sm)
        P.wait(cast_eng, sm, dma_done)
        P.wait(cast_eng, "T", self.last_mm[slot])
        if self.conv == "f32":
            P.op(cast_eng, lambda e, o=self.bf[slot][:, 0:n], i=self.raw[slot][:, 0:n]:
                 e.tensor_copy(o, i), inc="G")
        elif self.conv == "pm1":
            P.op(cast_eng, lambda e, o=self.bf[slot][:, 0:n], i=self.raw[slot][:, 0:n]:
                 e.tensor_scalar(o, i, 2.0, -1.0, ALU.mult, ALU.add), inc="G")
        else:
            P.op(cast_eng, lambda e, o=self.bf[slot][:, 0:n], i=self.raw[slot][:, 0:n],
                 j=self.raw2[slot][:, 0:n]: e.tensor_tensor(o, i, j, ALU.subtract), inc="G")
        self.cast_done[slot] = P.val("G")
        return self.cast_done[slot]

    def used(self, slot):
        self.last_mm[slot] = self.P.val("T")


def emit_fm_stage(P, env, q, QT, rhs, streams, noc, K, evac, sub_w=2048, rhs_wait=None):
    SUB = min(K, sub_w)
    nsub = K // SUB
    FREE = min(QT, 512)
    nth = QT // FREE
    ns = len(streams)
    psb = env["psb"]

    def ps_get(si, oc, th):
        return psb[si * 2 + th] if ns == 2 else psb[(oc % 2) * 2 + th]

    evac_done = env.setdefault("evac_done", {})
    for (strm, get_src) in streams:
        a, b = get_src(0, 0, SUB)
        strm.load(0, a, SUB, src_ap2=b)
    for oc in range(noc):
        for sub in range(nsub):
            idx = oc * nsub + sub
            cur = idx % 2
            if idx + 1 < noc * nsub:
                oc2, sub2 = divmod(idx + 1, nsub)
                k0, k1 = sub2 * SUB, (sub2 + 1) * SUB
                for (strm, get_src) in streams:
                    a, b = get_src(oc2, k0, k1)
                    strm.load((idx + 1) % 2, a, SUB, src_ap2=b)
            gv = max(strm.cast_done[cur] for strm, _ in streams)
            P.wait("t", "G", gv)
            if sub == 0:
                key = 0 if ns == 2 else oc % 2
                vv, ss = evac_done.get(key, (0, 0))
                P.wait("t", "V", vv)
                P.wait("t", "S", ss)
            if oc == 0 and rhs_wait is not None:
                for icg in range(SUB // 128):
                    rhs_wait(sub * (SUB // 128) + icg)
            for icl in range(SUB // 128):
                st_flag = (sub == 0 and icl == 0)
                sp_flag = (sub == nsub - 1 and icl == SUB // 128 - 1)
                for si in range(ns):
                    for th in range(nth):
                        P.op("t", lambda e, si=si, th=th, icl=icl, cur=cur, sub=sub,
                             st_flag=st_flag, sp_flag=sp_flag, oc=oc:
                             e.matmul(ps_get(si, oc, th)[:, 0:FREE],
                                      streams[si][0].bf[cur][:, icl * 128:(icl + 1) * 128],
                                      rhs(sub * (SUB // 128) + icl, th),
                                      start=st_flag, stop=sp_flag), inc="T")
            for (strm, _) in streams:
                strm.used(cur)
        tv = P.val("T")
        P.wait("v", "T", tv)
        P.wait("s", "T", tv)
        for th in range(nth):
            evac(oc, th, [ps_get(si, oc, th)[:, 0:FREE] for si in range(ns)])
        key = 0 if ns == 2 else oc % 2
        evac_done[key] = (P.val("V"), P.val("S"))


def emit_load_ln_transpose(P, env, x_h, row0, QT, out_xlt):
    xq, ln1, scr, ident, ps_t = env["xq"], env["ln1"], env["scr"], env["ident"], env["ps_t"]
    for tb in range(QT // 128):
        P.wait("y", "V")
        P.wait("y", "DM")
        P.op("y", lambda e, tb=tb: e.dma_start(
            out=xq[:], in_=x_h[row0 + tb * 128:row0 + (tb + 1) * 128, :]), inc="DM", n=16)
        P.wait("v", "DM")
        P.wait("s", "DM")
        emit_ln(P, xq[:], ln1[:], scr)
        P.wait("t", "V")
        for ic in range(8):
            slot = ps_t[:, (ic % 2) * 128:(ic % 2) * 128 + 128]
            P.wait("t", "S")
            P.wait("t", "V")
            P.op("t", lambda e, s=ln1[:, ic * 128:(ic + 1) * 128], o=slot:
                 e.transpose(o, s, ident[:]), inc="T")
            P.wait("s", "T")
            P.op("s", lambda e, o=out_xlt[:, ic, tb * 128:(tb + 1) * 128], s=slot:
                 e.copy(o, s), inc="S")
        P.wait("v", "S")
        P.wait("v", "T")
    P.wait("t", "S")


def emit_final_tile(P, env, x_h, y_h, row0, out_fm, tb):
    ident, ps_t, scr = env["ident"], env["ps_t"], env["ps_t2_scr"]
    ot, xq, yq = env["out_tm"], env["xq2"], env["yq"]
    for ic in range(8):
        slot = ps_t[:, (ic % 2) * 128:(ic % 2) * 128 + 128]
        P.wait("t", "V")
        P.wait("t", "S")
        P.op("t", lambda e, s=out_fm[:, ic, tb * 128:(tb + 1) * 128], o=slot:
             e.transpose(o, s, ident[:]), inc="T")
        P.wait("v", "T")
        P.op("v", lambda e, o=ot[:, ic * 128:(ic + 1) * 128], s=slot: e.tensor_copy(o, s), inc="V")
    P.wait("y", "V")
    P.wait("y", "DM")
    P.op("y", lambda e: e.dma_start(
        out=xq[:], in_=x_h[row0 + tb * 128:row0 + (tb + 1) * 128, :]), inc="DM", n=16)
    emit_ln(P, ot[:], ot[:], scr)
    P.wait("v", "DM")
    P.wait("v", "DO")
    P.op("v", lambda e: e.tensor_tensor(yq[:], ot[:], xq[:], ALU.add), inc="V")
    P.wait("g", "V")
    P.op("g", lambda e: e.dma_start(out=y_h[row0 + tb * 128:row0 + (tb + 1) * 128, :], in_=yq[:]),
         inc="DO", n=16)


def _mk_env(P, nc, ctx, QT, full=True):
    sb = lambda name, shape, dt=F32: ctx.enter_context(nc.sbuf_tensor(name, shape, dt))
    env = {}
    env["ident"] = sb("identity", [128, 128], F32)
    env["xq"] = sb("xq", [128, D], F32)
    env["xq2"] = sb("xq2", [128, D], F32)
    env["yq"] = sb("yq", [128, D], F32)
    env["ln1"] = sb("ln1", [128, D], F32)
    env["out_tm"] = sb("out_tm", [128, D], F32)
    env["cst"] = sb("cst", [128, 8], F32)
    env["scr"] = {"sq": sb("scr_sq", [128, D], F32),
                  "c0": sb("c0", [128, 1], F32), "c1": sb("c1", [128, 1], F32),
                  "c2": sb("c2", [128, 1], F32), "c3": sb("c3", [128, 1], F32),
                  "eps": env["cst"][:, 0:1]}
    if full:
        env["ps_t2_scr"] = {"sq": sb("scr_sq2", [128, D], F32),
                            "c0": sb("d0", [128, 1], F32), "c1": sb("d1", [128, 1], F32),
                            "c2": sb("d2", [128, 1], F32), "c3": sb("d3", [128, 1], F32),
                            "eps": env["cst"][:, 0:1]}
    env["xlt"] = sb("xlt", [128, 8, QT], BF16)
    env["ps_t"] = ctx.enter_context(nc.psum_tensor("ps_t", [128, 256], F32))
    return env


def build_l0ffn(TOK):
    nc = bass.Bass("TRN2", target_bir_lowering=False)
    x_h = nc.dram_tensor("x", [TOK, D], F32, kind="ExternalInput").ap()
    y_h = nc.dram_tensor("y", [TOK, D], F32, kind="ExternalOutput").ap()
    id_h = nc.dram_tensor("ident", [128, 128], F32, kind="ExternalInput").ap()
    sc_h = nc.dram_tensor("scl", [128, 144], F32, kind="ExternalInput").ap()
    w1_h = nc.dram_tensor("w1p", [32, 128, 1024], F32, kind="ExternalInput").ap()
    w2_h = nc.dram_tensor("w2p", [32, 128, 1024], F32, kind="ExternalInput").ap()
    h1_h = nc.dram_tensor("h1p", [32, 128, 4096], F32, kind="ExternalInput").ap()
    h2_h = nc.dram_tensor("h2p", [32, 128, 4096], F32, kind="ExternalInput").ap()
    w3_h = nc.dram_tensor("w3p", [8, 128, 4096], F32, kind="ExternalInput").ap()
    mid0 = nc.dram_tensor("mid0", [32, 128, TOK], BF16).ap()
    mid1 = nc.dram_tensor("mid1", [32, 128, TOK], BF16).ap()
    mid2 = nc.dram_tensor("mid2", [32, 128, TOK], BF16).ap()

    QT = 1024
    NQ = TOK // QT

    with ExitStack() as ctx:
        P = Plan(nc, ctx)
        sb = lambda name, shape, dt=F32: ctx.enter_context(nc.sbuf_tensor(name, shape, dt))
        env = _mk_env(P, nc, ctx, QT)
        scales = sb("scales", [128, 144], F32)
        rhs_buf = sb("rhsb", [128, 32, QT], BF16)
        out_fm = sb("out_fm", [128, 8, QT], F32)
        stg = sb("stg", [128, 512], BF16)
        ha_scr = sb("ha_scr", [128, 512], F32)
        wA = PanelStream(P, nc, ctx, "wA", 2048, "f32")
        wB = PanelStream(P, nc, ctx, "wB", 2048, "f32")
        env["psb"] = [ctx.enter_context(nc.psum_tensor(f"psb{i}", [128, 512], F32))
                      for i in range(4)]

        block = ctx.enter_context(nc.Block())
        P.op("y", lambda e: e.dma_start(out=env["ident"][:], in_=id_h), inc="DM", n=16)
        P.op("y", lambda e: e.dma_start(out=scales[:], in_=sc_h), inc="DM", n=16)
        for e_ in ("t", "v", "s", "g"):
            P.wait(e_, "DM")
        P.op("v", lambda e: e.memset(env["cst"][:], 0.0), inc="V")
        P.op("v", lambda e: e.memset(env["cst"][:, 0:1], LN_EPS), inc="V")

        s1a = lambda oc: scales[:, oc:oc + 1]
        s2a = lambda oc: scales[:, 32 + oc:32 + oc + 1]
        sh1a = lambda oc: scales[:, 64 + oc:64 + oc + 1]
        sh2a = lambda oc: scales[:, 96 + oc:96 + oc + 1]
        s3a = lambda oc: scales[:, 128 + oc:128 + oc + 1]

        lm_sems = [P.new_sem(f"LM{i}") for i in range(4)]
        lm_tot = [0, 0, 0, 0]

        def load_mid(mid, kc, q):
            P.wait("y", "T")
            P.wait("y", "DO")
            for ic in range(kc):
                g = ic * 4 // kc
                P.op("y", lambda e, ic=ic: e.dma_start(
                    out=rhs_buf[:, ic, 0:QT], in_=mid[ic, :, q * QT:(q + 1) * QT]), inc=f"LM{g}", n=16)
                lm_tot[g] = P.val(f"LM{g}")
            kcl = [kc]

            def rw(ic):
                g = ic * 4 // kcl[0]
                P.wait("t", f"LM{g}", lm_tot[g])
            return rw

        def dma_stage_out(mid, oc, q, th, writer):
            P.wait("g", "V" if writer == "v" else "S")
            P.op("g", lambda e, o=mid[oc, :, q * QT + th * 512:q * QT + (th + 1) * 512],
                 i=stg[:, 0:512]: e.dma_start(out=o, in_=i), inc="DO", n=16)
            P.wait("v", "DO")
            P.wait("s", "DO")

        for q in range(NQ):
            emit_load_ln_transpose(P, env, x_h, q * QT, QT, env["xlt"])
            rhs = lambda ic, th: env["xlt"][:, ic, th * 512:(th + 1) * 512]

            def evac1(oc, th, psl, q=q):
                P.op("s", lambda e, oc=oc: e.activation(ha_scr[:], psl[0][:], AF.Silu, scale=s1a(oc)), inc="S")
                P.wait("v", "S")
                P.op("v", lambda e, oc=oc: e.tensor_scalar(stg[:, 0:512], psl[1][:], s2a(oc), None, ALU.mult), inc="V")
                P.op("v", lambda e: e.tensor_tensor(stg[:, 0:512], ha_scr[:], stg[:, 0:512], ALU.mult), inc="V")
                dma_stage_out(mid0, oc, q, th, "v")

            emit_fm_stage(P, env, q, QT, rhs,
                          [(wA, lambda oc, k0, k1: (w1_h[oc][:, k0:k1], None)),
                           (wB, lambda oc, k0, k1: (w2_h[oc][:, k0:k1], None))],
                          32, 1024, evac1)

        for q in range(NQ):
            rw = load_mid(mid0, 32, q)
            rhs = lambda ic, th: rhs_buf[:, ic, th * 512:(th + 1) * 512]

            def evac2(oc, th, psl, q=q):
                P.op("s", lambda e, oc=oc: e.activation(stg[:, 0:512], psl[0][:], AF.Gelu, scale=sh1a(oc)), inc="S")
                dma_stage_out(mid1, oc, q, th, "s")

            emit_fm_stage(P, env, q, QT, rhs, [(wA, lambda oc, k0, k1: (h1_h[oc][:, k0:k1], None))],
                          32, 4096, evac2, rhs_wait=rw)

        for q in range(NQ):
            rw = load_mid(mid1, 32, q)
            rhs = lambda ic, th: rhs_buf[:, ic, th * 512:(th + 1) * 512]

            def evac3(oc, th, psl, q=q):
                P.op("s", lambda e, oc=oc: e.activation(stg[:, 0:512], psl[0][:], AF.Gelu, scale=sh2a(oc)), inc="S")
                dma_stage_out(mid2, oc, q, th, "s")

            emit_fm_stage(P, env, q, QT, rhs, [(wA, lambda oc, k0, k1: (h2_h[oc][:, k0:k1], None))],
                          32, 4096, evac3, rhs_wait=rw)

        for q in range(NQ):
            rw = load_mid(mid2, 32, q)
            rhs = lambda ic, th: rhs_buf[:, ic, th * 512:(th + 1) * 512]

            def evac4(oc, th, psl, q=q):
                P.op("v", lambda e, oc=oc, th=th: e.tensor_scalar(
                    out_fm[:, oc, th * 512:(th + 1) * 512], psl[0][:], s3a(oc), None, ALU.mult), inc="V")

            emit_fm_stage(P, env, q, QT, rhs, [(wA, lambda oc, k0, k1: (w3_h[oc][:, k0:k1], None))],
                          8, 4096, evac4, rhs_wait=rw)
            P.wait("t", "V")
            for tb in range(QT // 128):
                emit_final_tile(P, env, x_h, y_h, q * QT, out_fm, tb)

        P.wait("g", "DO")
        P.build(block)
    return nc


def build_l1ffn(TOK):
    nc = bass.Bass("TRN2", target_bir_lowering=False)
    x_h = nc.dram_tensor("x", [TOK, D], F32, kind="ExternalInput").ap()
    y_h = nc.dram_tensor("y", [TOK, D], F32, kind="ExternalOutput").ap()
    id_h = nc.dram_tensor("ident", [128, 128], F32, kind="ExternalInput").ap()
    on_h = nc.dram_tensor("ones1", [1, 128], F32, kind="ExternalInput").ap()
    s_h = [nc.dram_tensor(f"s{i}", [1, 1], F32, kind="ExternalInput").ap() for i in range(3)]
    k0a_h = nc.dram_tensor("k0a", [32, 128, 1024], U8, kind="ExternalInput").ap()
    k0b_h = nc.dram_tensor("k0b", [32, 128, 1024], U8, kind="ExternalInput").ap()
    k1_h = nc.dram_tensor("k1", [32, 128, 4096], U8, kind="ExternalInput").ap()
    k2_h = nc.dram_tensor("k2", [8, 128, 4096], U8, kind="ExternalInput").ap()
    mid0 = nc.dram_tensor("mid0", [32, 128, TOK], BF16).ap()
    mid1 = nc.dram_tensor("mid1", [32, 128, TOK], BF16).ap()

    QT = 1024
    NQ = TOK // QT

    with ExitStack() as ctx:
        P = Plan(nc, ctx)
        sb = lambda name, shape, dt=F32: ctx.enter_context(nc.sbuf_tensor(name, shape, dt))
        env = _mk_env(P, nc, ctx, QT)
        ones1 = sb("ones1s", [1, 128], F32)
        ssc = sb("ssc", [1, 4], F32)
        sc2 = sb("sc2", [128, 4], F32)
        rhs_buf = sb("rhsb", [128, 32, QT], BF16)
        out_fm = sb("out_fm", [128, 8, QT], F32)
        stg = sb("stg", [128, 512], BF16)
        wS = PanelStream(P, nc, ctx, "wS", 1024, "sub")
        wP = PanelStream(P, nc, ctx, "wP", 2048, "pm1")
        env["psb"] = [ctx.enter_context(nc.psum_tensor(f"psb{i}", [128, 512], F32))
                      for i in range(4)]

        block = ctx.enter_context(nc.Block())
        P.op("y", lambda e: e.dma_start(out=env["ident"][:], in_=id_h), inc="DM", n=16)
        P.op("y", lambda e: e.dma_start(out=ones1[:], in_=on_h), inc="DM", n=16)
        for i in range(3):
            P.op("y", lambda e, i=i: e.dma_start(out=ssc[:, i:i + 1], in_=s_h[i]), inc="DM", n=16)
        for e_ in ("t", "v", "s", "g"):
            P.wait(e_, "DM")
        P.op("v", lambda e: e.memset(env["cst"][:], 0.0))
        P.op("v", lambda e: e.memset(env["cst"][:, 0:1], LN_EPS))
        for i in range(3):
            P.op("t", lambda e, i=i: e.matmul(env["psb"][0][:, i:i + 1], ones1[:], ssc[:, i:i + 1],
                                              start=True, stop=True), inc="T")
        P.wait("s", "T")
        P.op("s", lambda e: e.activation(sc2[:, 0:3], env["psb"][0][:, 0:3], AF.Copy, scale=2.0))
        s2a = lambda i: sc2[:, i:i + 1]

        def load_mid(mid, kc, q):
            P.wait("y", "T")
            P.wait("y", "DO")
            for ic in range(kc):
                P.op("y", lambda e, ic=ic, q=q, mid=mid: e.dma_start(
                    out=rhs_buf[:, ic, 0:QT], in_=mid[ic, :, q * QT:(q + 1) * QT]), inc="DM", n=16)
            P.wait("t", "DM")

        def dma_stage_out(mid, oc, q, th):
            P.wait("g", "S")
            P.op("g", lambda e, o=mid[oc, :, q * QT + th * 512:q * QT + (th + 1) * 512],
                 i=stg[:, 0:512]: e.dma_start(out=o, in_=i), inc="DO", n=16)
            P.wait("v", "DO")
            P.wait("s", "DO")

        for q in range(NQ):
            emit_load_ln_transpose(P, env, x_h, q * QT, QT, env["xlt"])
            rhs = lambda ic, th: env["xlt"][:, ic, th * 512:(th + 1) * 512]

            def evac1(oc, th, psl, q=q):
                P.op("s", lambda e, oc=oc: e.activation(stg[:, 0:512], psl[0][:], AF.Relu, scale=s2a(0)), inc="S")
                dma_stage_out(mid0, oc, q, th)

            emit_fm_stage(P, env, q, QT, rhs,
                          [(wS, lambda oc, k0, k1: (k0a_h[oc][:, k0:k1], k0b_h[oc][:, k0:k1]))],
                          32, 1024, evac1)

        for q in range(NQ):
            rw = load_mid(mid0, 32, q)
            rhs = lambda ic, th: rhs_buf[:, ic, th * 512:(th + 1) * 512]

            def evac2(oc, th, psl, q=q):
                P.op("s", lambda e, oc=oc: e.activation(stg[:, 0:512], psl[0][:], AF.Relu, scale=s2a(1)), inc="S")
                dma_stage_out(mid1, oc, q, th)

            emit_fm_stage(P, env, q, QT, rhs, [(wP, lambda oc, k0, k1: (k1_h[oc][:, k0:k1], None))],
                          32, 4096, evac2)

        for q in range(NQ):
            rw = load_mid(mid1, 32, q)
            rhs = lambda ic, th: rhs_buf[:, ic, th * 512:(th + 1) * 512]

            def evac3(oc, th, psl, q=q):
                P.op("v", lambda e, oc=oc, th=th: e.tensor_scalar(
                    out_fm[:, oc, th * 512:(th + 1) * 512], psl[0][:], s2a(2), None, ALU.mult), inc="V")

            emit_fm_stage(P, env, q, QT, rhs, [(wP, lambda oc, k0, k1: (k2_h[oc][:, k0:k1], None))],
                          8, 4096, evac3)
            P.wait("t", "V")
            for tb in range(QT // 128):
                emit_final_tile(P, env, x_h, y_h, q * QT, out_fm, tb)

        P.wait("g", "DO")
        P.build(block)
    return nc


def l1ffn_launch(x_full, w):
    TOK = x_full.shape[0] // NCORES
    nc = build_l1ffn(TOK)
    k0 = np.asarray(w["k0"]).astype(np.uint8)
    k1 = np.asarray(w["k1"]).astype(np.uint8)
    k2 = np.asarray(w["k2"]).astype(np.uint8)
    shared = {
        "ident": _IDENT, "ones1": np.ones((1, 128), np.float32),
        "s0": np.asarray(w["s0"], np.float32).reshape(1, 1),
        "s1": np.asarray(w["s1"], np.float32).reshape(1, 1),
        "s2": np.asarray(w["s2"], np.float32).reshape(1, 1),
        "k0a": _sb_panels(k0[:1024]), "k0b": _sb_panels(k0[1024:]),
        "k1": _sb_panels(k1), "k2": _sb_panels(k2),
    }
    in_maps = [dict(shared, x=np.ascontiguousarray(x_full[i * TOK:(i + 1) * TOK]))
               for i in range(NCORES)]
    res = _run(nc, in_maps)
    return np.concatenate([r["y"] for r in res.results], axis=0)


def build_l0attn(NB=2, NT=2048):
    nc = bass.Bass("TRN2", target_bir_lowering=False)
    x_h = nc.dram_tensor("x", [NB * NT, D], F32, kind="ExternalInput").ap()
    y_h = nc.dram_tensor("y", [NB * NT, D], F32, kind="ExternalOutput").ap()
    id_h = nc.dram_tensor("ident", [128, 128], F32, kind="ExternalInput").ap()
    idb_h = nc.dram_tensor("identb", [128, 128], BF16, kind="ExternalInput").ap()
    on_h = nc.dram_tensor("ones1", [1, 128], F32, kind="ExternalInput").ap()
    onb_h = nc.dram_tensor("onesb", [128, 128], BF16, kind="ExternalInput").ap()
    mU_h = nc.dram_tensor("maskU", [128, 128], F32, kind="ExternalInput").ap()
    mL_h = nc.dram_tensor("maskL", [128, 128], F32, kind="ExternalInput").ap()
    mK_h = nc.dram_tensor("maskK", [64, 128], F32, kind="ExternalInput").ap()
    mA_h = nc.dram_tensor("maskA", [128, 64], F32, kind="ExternalInput").ap()
    mQ_h = nc.dram_tensor("maskQ", [64, 64], F32, kind="ExternalInput").ap()
    wq_h = nc.dram_tensor("wq", [8, 128, 1024], F32, kind="ExternalInput").ap()
    wk_h = nc.dram_tensor("wk", [8, 128, 1024], F32, kind="ExternalInput").ap()
    wv_h = nc.dram_tensor("wv", [8, 128, 1024], F32, kind="ExternalInput").ap()
    wo_h = nc.dram_tensor("wo", [8, 128, 1024], F32, kind="ExternalInput").ap()
    wf1_h = nc.dram_tensor("wf1", [1, 128, 1024], F32, kind="ExternalInput").ap()
    wf2_h = nc.dram_tensor("wf2", [128, 1024], F32, kind="ExternalInput").ap()
    wb_h = nc.dram_tensor("wbeta", [128, 64], F32, kind="ExternalInput").ap()
    wg1_h = nc.dram_tensor("wog1", [1, 128, 1024], F32, kind="ExternalInput").ap()
    wg2_h = nc.dram_tensor("wog2", [128, 1024], F32, kind="ExternalInput").ap()
    nw_h = nc.dram_tensor("normw", [1, 1024], F32, kind="ExternalInput").ap()
    odbg_h = nc.dram_tensor("odbg", [128, 1024], F32, kind="ExternalOutput").ap()

    SEG = 128
    CH = 64
    NSEG = NT // SEG

    with ExitStack() as ctx:
        P = Plan(nc, ctx)
        sb = lambda name, shape, dt=F32: ctx.enter_context(nc.sbuf_tensor(name, shape, dt))
        env = _mk_env(P, nc, ctx, SEG, full=False)
        ident_b = sb("identb_s", [128, 128], BF16)
        ones1 = sb("ones1s", [1, 128], F32)
        onesb = sb("onesb_s", [128, 128], BF16)
        masks = sb("masks", [128, 128 * 2 + 64 + 64 + 128], F32)
        mU = masks[:, 0:128]
        mL = masks[:, 128:256]
        mK = masks[0:64, 256:384]
        mA = masks[:, 384:448]
        mQ = masks[0:64, 448:512]
        wq_sb = sb("wq_sb", [128, 8, 1024], BF16)
        wk_sb = sb("wk_sb", [128, 8, 1024], BF16)
        wv_sb = sb("wv_sb", [128, 8, 1024], BF16)
        wo_sb = sb("wo_sb", [128, 8, 1024], BF16)
        wf1_sb = sb("wf1_sb", [128, 1024], BF16)
        wf2_sb = sb("wf2_sb", [128, 1024], BF16)
        wb_sb = sb("wb_sb", [128, 64], BF16)
        wg1_sb = sb("wg1_sb", [128, 1024], BF16)
        wg2_sb = sb("wg2_sb", [128, 1024], BF16)
        nw_sb = sb("nw_sb", [128, 1024], F32)
        wstage = sb("wstage", [128, 1024], F32)
        q_t = sb("q_t", [128, 8, SEG], F32)
        k_t = sb("k_t", [128, 8, SEG], F32)
        kh_t = sb("kh_t", [128, 8, SEG], F32)
        v_t = sb("v_t", [128, 8, SEG], BF16)
        sig_t = sb("sig_t", [128, 8, SEG], F32)
        wrec_t = sb("wrec_t", [128, 8, SEG], F32)
        E_t = sb("E_t", [128, 8, SEG], F32)
        Ei_t = sb("Ei_t", [128, 8, SEG], F32)
        Ep_t = sb("Ep_t", [128, 8, SEG], F32)
        Eip_t = sb("Eip_t", [128, 8, SEG], F32)
        kb_t = sb("kb_t", [128, 8, SEG], F32)
        At_t = sb("At_t", [128, 8, 2 * SEG], BF16)
        Bt_t = sb("Bt_t", [128, 8, 2 * SEG], BF16)
        kE_t = sb("kE_t", [128, 8, SEG], BF16)
        qE_t = sb("qE_t", [128, 8, SEG], BF16)
        f1_t = sb("f1_t", [128, 128], BF16)
        og1_t = sb("og1_t", [128, 128], BF16)
        brow = sb("brow", [1, 1024], F32)
        k2_t = sb("k2_t", [128, 8, SEG], BF16)
        rs_t = sb("rs_t", [128, 128], F32)
        o_tm = sb("o_tm", [128, 1024], F32)
        gate_t = sb("gate_t", [128, 1024], F32)
        yw_t = sb("yw_t", [128, 1024], BF16)
        ywT_t = sb("ywT_t", [128, 8, 128], BF16)
        ao_t = sb("ao_t", [128, 1024], F32)
        zero64 = sb("zero64", [128, 64], F32)
        S_f = [sb(f"S_f{h}", [128, 128], F32) for h in range(8)]
        S_b = [sb(f"S_b{h}", [128, 128], BF16) for h in range(8)]
        def tsk(name, shape, dt=BF16):
            return [sb(f"{name}{p}", shape, dt) for p in range(2)]
        LT_s = tsk("LT_s", [128, 128]); LN_s = tsk("LN_s", [128, 128])
        W_s = tsk("W_s", [128, 128]); LkT_s = tsk("LkT_s", [64, 128])
        Aqa_s = tsk("Aqa_s", [128, 64]); Aqk_s = tsk("Aqk_s", [64, 64])
        M0_s = tsk("M0_s", [128, 128]); M_s = tsk("M_s", [128, 128])
        Atm_s = tsk("Atm_s", [128, 128]); kEtm_s = tsk("kEtm_s", [64, 128])
        Vtm_s = tsk("Vtm_s", [64, 128])
        ps_g = [ctx.enter_context(nc.psum_tensor(f"ps_g{p}", [128, 512], F32)) for p in range(2)]
        ps_c = [ctx.enter_context(nc.psum_tensor(f"ps_c{p}", [128, 512], F32)) for p in range(2)]
        ps_m = [ctx.enter_context(nc.psum_tensor(f"ps_m{p}", [128, 512], F32)) for p in range(2)]
        ps_xs = ctx.enter_context(nc.psum_tensor("ps_xs", [128, 768], BF16))
        ps_x = [ps_xs[:, 0:384], ps_xs[:, 384:768]]
        ps_t = env["ps_t"]

        block = ctx.enter_context(nc.Block())

        P.op("y", lambda e: e.dma_start(out=env["ident"][:], in_=id_h), inc="DM", n=16)
        P.op("y", lambda e: e.dma_start(out=ident_b[:], in_=idb_h), inc="DM", n=16)
        P.op("y", lambda e: e.dma_start(out=ones1[:], in_=on_h), inc="DM", n=16)
        P.op("y", lambda e: e.dma_start(out=onesb[:], in_=onb_h), inc="DM", n=16)
        P.op("y", lambda e: e.dma_start(out=masks[:, 0:128], in_=mU_h), inc="DM", n=16)
        P.op("y", lambda e: e.dma_start(out=masks[:, 128:256], in_=mL_h), inc="DM", n=16)
        P.op("y", lambda e: e.dma_start(out=masks[0:64, 256:384], in_=mK_h), inc="DM", n=16)
        P.op("y", lambda e: e.dma_start(out=masks[:, 384:448], in_=mA_h), inc="DM", n=16)
        P.op("y", lambda e: e.dma_start(out=masks[0:64, 448:512], in_=mQ_h), inc="DM", n=16)
        for e_ in ("t", "v", "s", "g"):
            P.wait(e_, "DM")
        P.op("v", lambda e: e.memset(env["cst"][:], 0.0))
        P.op("v", lambda e: e.memset(env["cst"][:, 0:1], LN_EPS))
        P.op("v", lambda e: e.memset(env["cst"][:, 1:2], 1e-6))
        P.op("v", lambda e: e.memset(zero64[:], 0.0))
        eps6 = env["cst"][:, 1:2]

        def loadw(dst, src_ap, n):
            P.wait("y", "G")
            P.wait("y", "DM")
            P.op("y", lambda e: e.dma_start(out=wstage[:, 0:n], in_=src_ap), inc="DM", n=16)
            P.wait("g", "DM")
            P.op("g", lambda e: e.tensor_copy(dst, wstage[:, 0:n]))

        for i in range(8):
            loadw(wq_sb[:, i, :], wq_h[i], 1024)
            loadw(wk_sb[:, i, :], wk_h[i], 1024)
            loadw(wv_sb[:, i, :], wv_h[i], 1024)
            loadw(wo_sb[:, i, :], wo_h[i], 1024)
        loadw(wf1_sb[:], wf1_h[0], 1024)
        loadw(wf2_sb[:], wf2_h, 1024)
        loadw(wb_sb[:], wb_h, 64)
        loadw(wg1_sb[:], wg1_h[0], 1024)
        loadw(wg2_sb[:], wg2_h, 1024)
        P.wait("y", "G")
        P.wait("y", "DM")
        P.op("y", lambda e: e.dma_start(out=wstage[0:1, 0:1024], in_=nw_h), inc="DM", n=16)
        P.wait("t", "DM")
        P.wait("t", "G")
        for hlf in range(2):
            P.op("t", lambda e, hlf=hlf: e.matmul(ps_g[hlf][:, 0:512], ones1[:],
                 wstage[0:1, hlf * 512:(hlf + 1) * 512], start=True, stop=True), inc="T")
        P.wait("v", "T")
        P.op("v", lambda e: e.tensor_copy(nw_sb[:, 0:512], ps_g[0][:, 0:512]))
        P.op("v", lambda e: e.tensor_copy(nw_sb[:, 512:1024], ps_g[1][:, 0:512]))
        gv_init = P.val("G")
        P.wait("t", "G", gv_init)

        ln_scr = env["scr"]
        xlt = env["xlt"]
        last_o_read = {"v": 0}

        def transpose_to(dst_ap, src_ap, pslot, evac_eng="v", inb=False):
            P.wait("t", "V")
            P.wait("t", "S")
            P.op("t", lambda e: e.transpose(ps_x[pslot][:, 0:128], src_ap, ident_b[:]), inc="T")
            P.wait(evac_eng, "T")
            if evac_eng == "v":
                P.op("v", lambda e: e.tensor_copy(dst_ap, ps_x[pslot][:, 0:128]))
            else:
                P.op("s", lambda e: e.copy(dst_ap, ps_x[pslot][:, 0:128]))

        for b in range(NB):
            P.wait("v", "T")
            for h in range(8):
                P.op("v", lambda e, h=h: e.memset(S_f[h][:], 0.0))
                P.op("v", lambda e, h=h: e.memset(S_b[h][:], 0.0))
            for g in range(NSEG):
                row0 = b * NT + g * SEG
                emit_load_ln_transpose(P, env, x_h, row0, SEG, xlt)

                def proj8(w_sb, out_t, act, out_dt_bf=False):
                    for oc in range(8):
                        par = oc % 2
                        P.wait("t", "V")
                        P.wait("t", "S")
                        for ic in range(8):
                            P.op("t", lambda e, oc=oc, ic=ic, par=par: e.matmul(
                                ps_g[par][:, 0:128], w_sb[:, oc, ic * 128:(ic + 1) * 128],
                                xlt[:, ic, :], start=(ic == 0), stop=(ic == 7)), inc="T")
                        P.wait("s" if act else "v", "T")
                        if act == "silu":
                            P.op("s", lambda e, oc=oc, par=par: e.activation(
                                out_t[:, oc, :], ps_g[par][:, 0:128], AF.Silu))
                        elif act == "sigmoid":
                            P.op("s", lambda e, oc=oc, par=par: e.activation(
                                out_t[:, oc, :], ps_g[par][:, 0:128], AF.Sigmoid))
                        else:
                            P.op("v", lambda e, oc=oc, par=par: e.tensor_copy(
                                out_t[:, oc, :], ps_g[par][:, 0:128]))
                proj8(wq_sb, q_t, "silu")
                proj8(wk_sb, k_t, "silu")
                proj8(wv_sb, v_t, "silu")
                P.wait("t", "V")
                P.wait("t", "S")
                for ic in range(8):
                    P.op("t", lambda e, ic=ic: e.matmul(ps_c[0][:, 0:128],
                         wf1_sb[:, ic * 128:(ic + 1) * 128], xlt[:, ic, :],
                         start=(ic == 0), stop=(ic == 7)), inc="T")
                P.wait("v", "T")
                P.op("v", lambda e: e.tensor_copy(f1_t[:], ps_c[0][:, 0:128]))
                vv = P.val("V")
                P.wait("t", "V", vv)
                for oc in range(8):
                    P.op("t", lambda e, oc=oc: e.matmul(ps_c[oc % 2][:, 128:256],
                         wf2_sb[:, oc * 128:(oc + 1) * 128], f1_t[:],
                         start=True, stop=True), inc="T")
                    P.wait("s", "T")
                    P.op("s", lambda e, oc=oc: e.activation(sig_t[:, oc, :],
                         ps_c[oc % 2][:, 128:256], AF.Sigmoid))
                    if oc < 7:
                        P.wait("t", "S")
                P.wait("t", "V")
                P.wait("t", "S")
                for ic in range(8):
                    P.op("t", lambda e, ic=ic: e.matmul(ps_c[0][:, 256:384],
                         wg1_sb[:, ic * 128:(ic + 1) * 128], xlt[:, ic, :],
                         start=(ic == 0), stop=(ic == 7)), inc="T")
                P.wait("v", "T")
                P.op("v", lambda e: e.tensor_copy(og1_t[:], ps_c[0][:, 256:384]))
                for h in range(8):
                    for ic in range(8):
                        P.op("t", lambda e, h=h, ic=ic: e.matmul(
                            ps_m[h // 4][0:1, (h % 4) * 128:(h % 4) * 128 + 128],
                            wb_sb[:, ic * 8 + h:ic * 8 + h + 1], xlt[:, ic, :],
                            start=(ic == 0), stop=(ic == 7)), inc="T")
                P.wait("s", "T")
                P.op("s", lambda e: e.activation(brow[0:1, 0:512], ps_m[0][0:1, 0:512], AF.Sigmoid))
                P.op("s", lambda e: e.activation(brow[0:1, 512:1024], ps_m[1][0:1, 0:512], AF.Sigmoid))
                P.wait("v", "S")
                P.op("v", lambda e: e.tensor_scalar(brow[:], brow[:], -2.0, None, ALU.mult))
                vv = P.val("V")
                P.wait("t", "V", vv)
                P.wait("t", "S")
                for hlf in range(2):
                    P.op("t", lambda e, hlf=hlf: e.matmul(ps_m[hlf][:, 0:512], ones1[:],
                         brow[0:1, hlf * 512:(hlf + 1) * 512], start=True, stop=True), inc="T")
                P.wait("s", "V")
                P.op("s", lambda e: e.activation(k2_t[:, :, :], k_t[:, :, :], AF.Square))
                P.wait("t", "S")
                for h in range(8):
                    par = h % 2
                    P.op("t", lambda e, h=h, par=par: e.matmul(ps_g[par][:, 128:256], onesb[:],
                         k2_t[:, h, :], start=True, stop=True), inc="T")
                    P.wait("s", "T")
                    P.op("s", lambda e, par=par: e.activation(rs_t[:], ps_g[par][:, 128:256],
                         AF.Ln, bias=eps6))
                    P.op("s", lambda e: e.activation(rs_t[:], rs_t[:], AF.Exp, scale=-0.5))
                    P.wait("v", "S")
                    P.op("v", lambda e, h=h: e.tensor_tensor(kh_t[:, h, :], k_t[:, h, :],
                         rs_t[:], ALU.mult))
                    if h < 7:
                        P.wait("t", "V")
                        P.wait("t", "S")
                P.wait("v", "T")
                for hlf in range(2):
                    P.op("v", lambda e, hlf=hlf: e.tensor_tensor(
                        kb_t[:, 4 * hlf:4 * hlf + 4, :], kh_t[:, 4 * hlf:4 * hlf + 4, :],
                        ps_m[hlf][:, 0:512].rearrange("p (h t) -> p h t", h=4), ALU.mult))
                P.op("v", lambda e: e.reciprocal(wrec_t[:, :, :], sig_t[:, :, :]))
                for h in range(8):
                    for c in range(SEG // CH):
                        sl = (slice(None), h, slice(c * CH, (c + 1) * CH))
                        P.op("v", lambda e, sl=sl: e.tensor_tensor_scan(
                            E_t[sl], sig_t[sl], zero64[:], 1.0, ALU.mult, ALU.add))
                        P.op("v", lambda e, sl=sl: e.tensor_tensor_scan(
                            Ei_t[sl], wrec_t[sl], zero64[:], 1.0, ALU.mult, ALU.add))
                P.op("v", lambda e: e.tensor_tensor(Ep_t[:, :, :], E_t[:, :, :], wrec_t[:, :, :], ALU.mult))
                P.op("v", lambda e: e.tensor_tensor(Eip_t[:, :, :], Ei_t[:, :, :], sig_t[:, :, :], ALU.mult))
                def ilv(dst, ph):
                    t = dst.tensor if hasattr(dst, "tensor") else dst
                    return bass.AP(t, ph, [[2 * SEG * 8 // 8 * 8, 128], [2 * SEG, 8], [2, SEG]])
                def ilv2(handle, ph):
                    return bass.AP(handle, ph, [[8 * 2 * SEG, 128], [2 * SEG, 8], [2, SEG]])
                P.op("v", lambda e: e.tensor_tensor(ilv2(At_t, 0), kb_t[:, :, :], Eip_t[:, :, :], ALU.mult))
                P.op("v", lambda e: e.tensor_tensor(ilv2(At_t, 1), kb_t[:, :, :], Ei_t[:, :, :], ALU.mult))
                P.op("v", lambda e: e.tensor_tensor(ilv2(Bt_t, 0), kh_t[:, :, :], Ep_t[:, :, :], ALU.mult))
                P.op("v", lambda e: e.tensor_tensor(ilv2(Bt_t, 1), kh_t[:, :, :], E_t[:, :, :], ALU.mult))
                P.op("v", lambda e: e.tensor_tensor(kE_t[:, :, :], kh_t[:, :, :], Ei_t[:, :, :], ALU.mult))
                P.op("v", lambda e: e.tensor_tensor(qE_t[:, :, :], q_t[:, :, :], E_t[:, :, :], ALU.mult))
                seg_ready_v = P.val("V")
                seg_ready_s = P.val("S")

                def task_stages(c, h, par):
                    At = At_t[:, h, c * 128:(c + 1) * 128]
                    Bt = Bt_t[:, h, c * 128:(c + 1) * 128]
                    kE = kE_t[:, h, c * CH:(c + 1) * CH]
                    qE = qE_t[:, h, c * CH:(c + 1) * CH]
                    Vc = v_t[:, h, c * CH:(c + 1) * CH]
                    Elast = E_t[:, h, (c + 1) * CH - 1:(c + 1) * CH]
                    stages = []

                    def s_grams():
                        P.wait("t", "V")
                        P.wait("t", "S")
                        P.op("t", lambda e: e.matmul(ps_g[par][:, 0:128], At, Bt, start=True, stop=True), inc="T")
                        P.op("t", lambda e: e.matmul(ps_g[par][:, 128:256], Bt, At, start=True, stop=True), inc="T")
                        P.op("t", lambda e: e.matmul(ps_g[par][0:64, 256:384], kE, Bt, start=True, stop=True), inc="T")
                        P.op("t", lambda e: e.matmul(ps_g[par][:, 384:448], At, qE, start=True, stop=True), inc="T")
                        P.op("t", lambda e: e.matmul(ps_g[par][0:64, 448:512], kE, qE, start=True, stop=True), inc="T")
                        P.op("t", lambda e: e.transpose(ps_x[par][:, 0:128], At, ident_b[:]), inc="T")
                        P.op("t", lambda e: e.transpose(ps_x[par][0:64, 128:256], kE, ident_b[:]), inc="T")
                        P.op("t", lambda e: e.transpose(ps_x[par][0:64, 256:384], Vc, ident_b[:]), inc="T")
                        tg = P.val("T")
                        P.wait("v", "T", tg)
                        P.wait("s", "T", tg)
                        P.op("v", lambda e: e.tensor_tensor(LT_s[par][:], ps_g[par][:, 0:128], mU, ALU.mult))
                        P.op("v", lambda e: e.tensor_tensor(LN_s[par][:], ps_g[par][:, 128:256], mL, ALU.mult))
                        P.op("v", lambda e: e.tensor_tensor(LkT_s[par][:], ps_g[par][0:64, 256:384], mK, ALU.mult))
                        P.op("v", lambda e: e.tensor_tensor(Aqa_s[par][:], ps_g[par][:, 384:448], mA, ALU.mult))
                        P.op("v", lambda e: e.tensor_tensor(Aqk_s[par][:], ps_g[par][0:64, 448:512], mQ, ALU.mult))
                        P.op("s", lambda e: e.copy(Atm_s[par][:], ps_x[par][:, 0:128]))
                        P.op("s", lambda e: e.copy(kEtm_s[par][:], ps_x[par][0:64, 128:256]))
                        P.op("s", lambda e: e.copy(Vtm_s[par][:], ps_x[par][0:64, 256:384]))
                        P.op("v", lambda e: e.tensor_tensor(W_s[par][:], LT_s[par][:], ident_b[:], ALU.add))
                        task_state[par] = (P.val("V"), P.val("S"))
                    stages.append(s_grams)

                    for it in range(6):
                        def s_chain(it=it):
                            wv_, sv_ = task_state[par]
                            P.wait("t", "V", wv_)
                            P.op("t", lambda e: e.matmul(ps_c[par][:, 128:256], LN_s[par][:], LT_s[par][:], start=True, stop=True), inc="T")
                            P.op("t", lambda e: e.matmul(ps_c[par][:, 256:384], LT_s[par][:], LN_s[par][:], start=True, stop=True), inc="T")
                            tt = P.val("T")
                            P.wait("v", "T", tt)
                            P.op("v", lambda e: e.tensor_copy(LT_s[par][:], ps_c[par][:, 128:256]))
                            P.op("v", lambda e: e.tensor_copy(LN_s[par][:], ps_c[par][:, 256:384]))
                            wv_ = P.val("V")
                            P.wait("t", "V", wv_)
                            P.op("t", lambda e: e.matmul(ps_c[par][:, 0:128], LN_s[par][:], W_s[par][:], start=True, stop=True), inc="T")
                            tt = P.val("T")
                            P.wait("v", "T", tt)
                            P.op("v", lambda e: e.tensor_tensor(W_s[par][:], ps_c[par][:, 0:128], W_s[par][:], ALU.add))
                            task_state[par] = (P.val("V"), task_state[par][1])
                        stages.append(s_chain)

                    def s_m0():
                        wv_, sv_ = task_state[par]
                        P.wait("t", "V", wv_)
                        P.wait("t", "S", sv_)
                        P.op("t", lambda e: e.matmul(ps_m[par][:, 0:128], Bt, S_b[h][:], start=True, stop=False), inc="T")
                        P.op("t", lambda e: e.matmul(ps_m[par][:, 0:128], LkT_s[par][:], Vtm_s[par][:], start=False, stop=True), inc="T")
                        tt = P.val("T")
                        P.wait("v", "T", tt)
                        P.op("v", lambda e: e.tensor_copy(M0_s[par][:], ps_m[par][:, 0:128]))
                        task_state[par] = (P.val("V"), sv_)
                    stages.append(s_m0)

                    def s_m():
                        wv_, sv_ = task_state[par]
                        P.wait("t", "V", wv_)
                        P.op("t", lambda e: e.matmul(ps_m[par][:, 128:256], W_s[par][:], M0_s[par][:], start=True, stop=True), inc="T")
                        tt = P.val("T")
                        P.wait("v", "T", tt)
                        P.op("v", lambda e: e.tensor_copy(M_s[par][:], ps_m[par][:, 128:256]))
                        task_state[par] = (P.val("V"), sv_)
                    stages.append(s_m)

                    def s_out():
                        wv_, sv_ = task_state[par]
                        P.wait("t", "V", wv_)
                        P.op("t", lambda e: e.matmul(ps_m[par][0:64, 256:384], qE, S_b[h][:], start=True, stop=False), inc="T")
                        P.op("t", lambda e: e.matmul(ps_m[par][0:64, 256:384], Aqa_s[par][:], M_s[par][:], start=False, stop=False), inc="T")
                        P.op("t", lambda e: e.matmul(ps_m[par][0:64, 256:384], Aqk_s[par][:], Vtm_s[par][:], start=False, stop=True), inc="T")
                        P.op("t", lambda e: e.matmul(ps_m[par][:, 384:512][:, 0:128], Atm_s[par][:], M_s[par][:], start=True, stop=False), inc="T")
                        P.op("t", lambda e: e.matmul(ps_m[par][:, 384:512][:, 0:128], kEtm_s[par][:], Vtm_s[par][:], start=False, stop=True), inc="T")
                        tt = P.val("T")
                        P.wait("v", "T", tt)
                        P.op("v", lambda e: e.tensor_copy(
                            o_tm[c * CH:(c + 1) * CH, h * 128:(h + 1) * 128] if c == 0 else
                            o_tm[64:128, h * 128:(h + 1) * 128], ps_m[par][0:64, 256:384]))
                        P.op("v", lambda e: e.tensor_tensor(S_f[h][:], S_f[h][:], ps_m[par][:, 384:512][:, 0:128], ALU.add))
                        P.op("v", lambda e: e.tensor_scalar(S_f[h][:], S_f[h][:], Elast, None, ALU.mult))
                        P.op("v", lambda e: e.tensor_copy(S_b[h][:], S_f[h][:]))
                    stages.append(s_out)
                    return stages

                task_state = {}
                tasks = [(c, h) for c in range(SEG // CH) for h in range(8)]
                for pi in range(0, len(tasks), 2):
                    (cA, hA) = tasks[pi]
                    (cB, hB) = tasks[pi + 1]
                    stA = task_stages(cA, hA, 0)
                    stB = task_stages(cB, hB, 1)
                    for sA, sB in zip(stA, stB):
                        sA()
                        sB()

                if NB * NT <= 256:
                    P.wait("g", "V")
                    P.wait("g", "DO")
                    P.op("g", lambda e: e.dma_start(out=odbg_h, in_=o_tm[:]), inc="DO", n=16)
                    P.wait("v", "DO")
                P.wait("t", "V")
                P.wait("t", "S")
                for hlf in range(2):
                    P.op("t", lambda e, hlf=hlf: e.matmul(ps_g[hlf][:, 256:384].tensor.ap()[:, hlf * 0:0 + 512] if False else ps_g[hlf][:, 0:512],
                         og1_t[:], wg2_sb[:, hlf * 512:(hlf + 1) * 512], start=True, stop=True), inc="T")
                P.wait("s", "T")
                for hlf in range(2):
                    P.op("s", lambda e, hlf=hlf: e.activation(gate_t[:, hlf * 512:(hlf + 1) * 512],
                         ps_g[hlf][:, 0:512], AF.Sigmoid))
                P.wait("v", "S")
                P.op("v", lambda e: e.tensor_tensor(o_tm[:], o_tm[:], gate_t[:], ALU.mult))
                P.op("v", lambda e: e.tensor_tensor(gate_t[:], o_tm[:], o_tm[:], ALU.mult))
                P.op("v", lambda e: e.tensor_reduce(rs_t[:, 0:8].rearrange("p (h x) -> p h x", x=1),
                     gate_t[:].rearrange("p (h d) -> p h d", h=8), AX.X, ALU.add))
                P.wait("s", "V")
                P.op("s", lambda e: e.activation(rs_t[:, 0:8], rs_t[:, 0:8], AF.Ln, scale=1.0 / 128.0, bias=eps6))
                P.op("s", lambda e: e.activation(rs_t[:, 0:8], rs_t[:, 0:8], AF.Exp, scale=-0.5))
                P.wait("v", "S")
                P.op("v", lambda e: e.tensor_tensor(o_tm[:].rearrange("p (h d) -> p h d", h=8),
                     o_tm[:].rearrange("p (h d) -> p h d", h=8),
                     bass.AP(rs_t, 0, [[128 if False else rs_t.shape[1], 128], [1, 8], [0, 128]]),
                     ALU.mult))
                P.op("v", lambda e: e.tensor_tensor(yw_t[:], o_tm[:], nw_sb[:], ALU.mult))
                for ic in range(8):
                    transpose_to(ywT_t[:, ic, :], yw_t[:, ic * 128:(ic + 1) * 128], ic % 2, "s")
                sv2 = P.val("S")
                P.wait("t", "S", sv2)
                P.wait("t", "V")
                for oc in range(8):
                    for ic in range(8):
                        P.op("t", lambda e, oc=oc, ic=ic: e.matmul(
                             ps_m[oc % 2][:, (oc // 2) * 128:(oc // 2) * 128 + 128],
                             ywT_t[:, ic, :], wo_sb[:, oc, ic * 128:(ic + 1) * 128],
                             start=(ic == 0), stop=(ic == 7)), inc="T")
                P.wait("v", "T")
                for oc in range(8):
                    P.op("v", lambda e, oc=oc: e.tensor_copy(ao_t[:, oc * 128:(oc + 1) * 128],
                         ps_m[oc % 2][:, (oc // 2) * 128:(oc // 2) * 128 + 128]))
                emit_ln(P, ao_t[:], ao_t[:], ln_scr)
                P.wait("y", "V")
                P.wait("y", "DM")
                P.op("y", lambda e, row0=row0: e.dma_start(out=env["xq2"][:], in_=x_h[row0:row0 + 128, :]), inc="DM", n=16)
                P.wait("v", "DM")
                P.wait("v", "DO")
                P.op("v", lambda e: e.tensor_tensor(env["yq"][:], ao_t[:], env["xq2"][:], ALU.add))
                P.wait("g", "V")
                P.op("g", lambda e, row0=row0: e.dma_start(out=y_h[row0:row0 + 128, :], in_=env["yq"][:]), inc="DO", n=16)

        P.wait("g", "DO")
        P.build(block)
    return nc


def l0attn_launch(x_full, w, ncores=NCORES):
    nb, nt, _ = x_full.shape
    nc = build_l0attn(nb // ncores, nt)
    n = 128
    iU = np.arange(128)
    maskU = (iU[None, :] > iU[:, None]).astype(np.float32)
    maskL = (iU[:, None] > iU[None, :]).astype(np.float32)
    j64 = np.arange(64)
    maskK = (iU[None, :] > (2 * j64 + 1)[:, None]).astype(np.float32)
    maskA = ((iU[:, None]) <= (2 * j64 + 1)[None, :]).astype(np.float32)
    maskQ = (j64[None, :] >= j64[:, None]).astype(np.float32)
    import ml_dtypes
    shared = {
        "ident": _IDENT, "identb": _IDENT.astype(ml_dtypes.bfloat16),
        "ones1": np.ones((1, 128), np.float32),
        "onesb": np.ones((128, 128), ml_dtypes.bfloat16),
        "maskU": maskU, "maskL": maskL, "maskK": maskK, "maskA": maskA, "maskQ": maskQ,
        "wq": _sb_panels(np.asarray(w["Wq"], np.float32)),
        "wk": _sb_panels(np.asarray(w["Wk"], np.float32)),
        "wv": _sb_panels(np.asarray(w["Wv"], np.float32)),
        "wo": _sb_panels(np.asarray(w["Wo"], np.float32)),
        "wf1": _sb_panels(np.asarray(w["Wf1"], np.float32)),
        "wf2": np.ascontiguousarray(np.asarray(w["Wf2"], np.float32)),
        "wbeta": np.ascontiguousarray(np.asarray(w["Wbeta"], np.float32).reshape(8, 128, 8)
                                      .transpose(1, 0, 2).reshape(128, 64)),
        "wog1": _sb_panels(np.asarray(w["Wog1"], np.float32)),
        "wog2": np.ascontiguousarray(np.asarray(w["Wog2"], np.float32)),
        "normw": np.ascontiguousarray(np.asarray(w["norm_w"], np.float32).reshape(1, 1024)),
    }
    nbc = nb // ncores
    in_maps = [dict(shared, x=np.ascontiguousarray(x_full[nbc * i:nbc * (i + 1)].reshape(nbc * nt, D)))
               for i in range(ncores)]
    res = run_bass_kernel_spmd(nc, in_maps, core_ids=list(range(ncores)))
    return np.stack([r["y"].reshape(nbc, nt, D) for r in res.results]).reshape(nb, nt, D)


def emit_uhg_norm(P, x_ap, out_ap, scr, eps_add, sq_w=D):
    sq, c0, c1 = scr["sq"], scr["c0"][:], scr["c1"][:]
    P.wait("s", "V")
    P.op("s", lambda e, x=x_ap: e.activation(sq[:], x, AF.Square, accum_out=c1), inc="S")
    P.op("s", lambda e: e.activation(c1, c1, AF.Ln), inc="S")
    P.op("s", lambda e: e.activation(c1, c1, AF.Exp, scale=0.5), inc="S")
    P.wait("v", "S")
    P.op("v", lambda e: e.tensor_scalar(c1, c1, eps_add, None, ALU.add))
    P.op("v", lambda e: e.reciprocal(c1, c1))
    P.op("v", lambda e, x=x_ap, o=out_ap: e.tensor_scalar(o, x, c1, None, ALU.mult))


def emit_uhg_ln(P, x_ap, out_ap, scr, g_bc, b_bc, width=D):
    sq, c0, c1, c2, c3 = (scr[k][:] for k in ("sq", "c0", "c1", "c2", "c3"))
    P.wait("s", "V")
    P.op("v", lambda e, x=x_ap: e.tensor_reduce(c0, x, AX.X, ALU.add))
    P.op("s", lambda e, x=x_ap: e.activation(sq, x, AF.Square, accum_out=c1), inc="S")
    P.op("v", lambda e: e.tensor_tensor(c2, c0, c0, ALU.mult))
    P.wait("v", "S")
    P.op("v", lambda e: e.scalar_tensor_tensor(c3, c2, -1.0 / width, c1, ALU.mult, ALU.add))
    P.op("v", lambda e: e.tensor_scalar(c0, c0, 1.0 / width, None, ALU.mult))
    P.wait("s", "V")
    P.op("s", lambda e: e.activation(c3, c3, AF.Ln, scale=1.0 / (width - 1), bias=scr["eps"]), inc="S")
    P.op("s", lambda e: e.activation(c3, c3, AF.Exp, scale=-0.5), inc="S")
    P.wait("v", "S")
    P.op("v", lambda e, x=x_ap, o=out_ap: e.tensor_scalar(o, x, c0, c3, ALU.subtract, ALU.mult))
    P.op("v", lambda e, o=out_ap: e.tensor_tensor(o, o, g_bc, ALU.mult))
    P.op("v", lambda e, o=out_ap: e.tensor_tensor(o, o, b_bc, ALU.add))


def build_l1attn(BC=256):
    nc = bass.Bass("TRN2", target_bir_lowering=False)
    S = 16
    TOK = S * BC
    x_h = nc.dram_tensor("x", [TOK, D], F32, kind="ExternalInput").ap()
    y_h = nc.dram_tensor("y", [TOK, D], F32, kind="ExternalOutput").ap()
    id_h = nc.dram_tensor("ident", [128, 128], F32, kind="ExternalInput").ap()
    idb_h = nc.dram_tensor("identb", [128, 128], BF16, kind="ExternalInput").ap()
    on_h = nc.dram_tensor("ones1", [1, 128], F32, kind="ExternalInput").ap()
    onb_h = nc.dram_tensor("onesb", [128, 128], BF16, kind="ExternalInput").ap()
    wqt_h = nc.dram_tensor("wqt", [8, 128, 1024], F32, kind="ExternalInput").ap()
    wkt_h = nc.dram_tensor("wkt", [8, 128, 1024], F32, kind="ExternalInput").ap()
    wvt_h = nc.dram_tensor("wvt", [8, 128, 1024], F32, kind="ExternalInput").ap()
    wot_h = nc.dram_tensor("wot", [8, 128, 1024], F32, kind="ExternalInput").ap()
    w1p_h = nc.dram_tensor("w1p", [32, 128, 1024], F32, kind="ExternalInput").ap()
    w2p_h = nc.dram_tensor("w2p", [8, 128, 4096], F32, kind="ExternalInput").ap()
    brows_h = nc.dram_tensor("brows", [9, 1024], F32, kind="ExternalInput").ap()
    b1c_h = nc.dram_tensor("b1c", [128, 32], F32, kind="ExternalInput").ap()

    NSL = BC // 128

    with ExitStack() as ctx:
        P = Plan(nc, ctx)
        sb = lambda name, shape, dt=F32: ctx.enter_context(nc.sbuf_tensor(name, shape, dt))
        env = _mk_env(P, nc, ctx, 128, full=False)
        ident_b = sb("identb_s", [128, 128], BF16)
        ones1 = sb("ones1s", [1, 128], F32)
        onesb = sb("onesb_s", [128, 128], BF16)
        bias_bc = sb("bias_bc", [128, 9, 1024], BF16)
        b1c = sb("b1c_s", [128, 32], F32)
        brow_st = sb("brow_st", [1, 1024], F32)
        wproj = sb("wproj", [128, 8, 1024], BF16)
        wstage = sb("wstage2", [128, 1024], F32)
        srcT = sb("srcT_s", [128, 8, 128], BF16)
        kv_bf = sb("kv_bf", [128, 16, 1024], BF16)
        kn_all = sb("kn_all", [128, 16, 16], F32)
        qn_all = sb("qn_all", [128, 16, 16], F32)
        dots = sb("dots", [128, 16, 16, 16], BF16)
        attn = sb("attn", [128, 16, 16, 16], BF16)
        x1_all = sb("x1_all", [128, 16, 1024], BF16)
        qs = sb("qs", [128, 1024], BF16)
        zs = sb("zs", [128, 1024], F32)
        acc = sb("acc", [128, 1024], F32)
        aT = sb("aT", [128, 8, 128], BF16)
        sm1 = sb("sm1", [128, 256], F32)
        sm2 = sb("sm2", [128, 256], F32)
        smr = sb("smr", [128, 16], F32)
        x1T = sb("x1T", [128, 8, 128], BF16)
        hbuf = sb("hbuf", [128, 32, 128], BF16)
        hsq = sb("hsq", [128, 32, 128], BF16)
        ffm = sb("ffm", [128, 8, 128], BF16)
        wA = PanelStream(P, nc, ctx, "wA1", 1024, "f32")
        psb = [ctx.enter_context(nc.psum_tensor(f"psc{i}", [128, 512], F32)) for i in range(4)]
        env["psb"] = psb
        ps_t = env["ps_t"]
        ps_xb = ctx.enter_context(nc.psum_tensor("ps_xb", [128, 256], BF16))

        block = ctx.enter_context(nc.Block())
        scr = env["scr"]
        P.op("y", lambda e: e.dma_start(out=env["ident"][:], in_=id_h), inc="DM", n=16)
        P.op("y", lambda e: e.dma_start(out=ident_b[:], in_=idb_h), inc="DM", n=16)
        P.op("y", lambda e: e.dma_start(out=ones1[:], in_=on_h), inc="DM", n=16)
        P.op("y", lambda e: e.dma_start(out=onesb[:], in_=onb_h), inc="DM", n=16)
        P.op("y", lambda e: e.dma_start(out=b1c[:], in_=b1c_h), inc="DM", n=16)
        for e_ in ("t", "v", "s", "g"):
            P.wait(e_, "DM")
        P.op("v", lambda e: e.memset(env["cst"][:], 0.0))
        P.op("v", lambda e: e.memset(env["cst"][:, 0:1], LN_EPS))
        P.op("v", lambda e: e.memset(env["cst"][:, 2:3], 1e-38))
        for r in range(9):
            P.wait("y", "DM")
            P.wait("y", "V")
            P.op("y", lambda e, r=r: e.dma_start(out=brow_st[:], in_=brows_h[r:r + 1, :]), inc="DM", n=16)
            P.wait("t", "DM")
            P.wait("t", "V")
            for hlf in range(2):
                P.op("t", lambda e, hlf=hlf: e.matmul(psb[hlf][:, 0:512], ones1[:],
                     brow_st[0:1, hlf * 512:(hlf + 1) * 512], start=True, stop=True), inc="T")
            P.wait("v", "T")
            for hlf in range(2):
                P.op("v", lambda e, r=r, hlf=hlf: e.tensor_copy(
                    bias_bc[:, r, hlf * 512:(hlf + 1) * 512], psb[hlf][:, 0:512]))
        BQ, BK, BV, BO, B2, G1, BE1, G2, BE2 = range(9)

        def loadw_phase(w_h):
            for i in range(8):
                P.wait("y", "G")
                P.wait("y", "DM")
                P.op("y", lambda e, i=i: e.dma_start(out=wstage[:], in_=w_h[i]), inc="DM", n=16)
                P.wait("g", "DM")
                P.wait("g", "T")
                P.op("g", lambda e, i=i: e.tensor_copy(wproj[:, i, :], wstage[:]))
            P.wait("t", "G")

        def ln_src(i, sl, out_ap):
            P.wait("y", "V")
            P.wait("y", "DM")
            row = i * BC + sl * 128
            P.op("y", lambda e, row=row: e.dma_start(out=env["xq"][:], in_=x_h[row:row + 128, :]), inc="DM", n=16)
            P.wait("v", "DM")
            P.wait("s", "DM")
            emit_ln(P, env["xq"][:], out_ap, scr)

        def mk_srcT(i, sl):
            ln_src(i, sl, zs[:])
            P.wait("t", "V")
            for ic in range(8):
                P.wait("t", "S")
                P.op("t", lambda e, ic=ic: e.transpose(
                    ps_t[:, (ic % 2) * 128:(ic % 2) * 128 + 128],
                    zs[:, ic * 128:(ic + 1) * 128], env["ident"][:]), inc="T")
                P.wait("s", "T")
                P.op("s", lambda e, ic=ic: e.copy(
                    srcT[:, ic, :], ps_t[:, (ic % 2) * 128:(ic % 2) * 128 + 128]), inc="S")
            P.wait("v", "S")
            P.wait("t", "S")

        def proj_tm(srcT_i, par, oc_half):
            for ic in range(8):
                P.op("t", lambda e, ic=ic, par=par, oc_half=oc_half, srcT_i=srcT_i: e.matmul(
                    psb[par][:, 0:512], srcT_i[:, ic, :],
                    wproj[:, ic, oc_half * 512:(oc_half + 1) * 512],
                    start=(ic == 0), stop=(ic == 7)), inc="T")

        def uhg_project(srcT_i, bias_row, out_ap):
            P.wait("t", "V")
            P.wait("t", "S")
            proj_tm(srcT_i, 0, 0)
            proj_tm(srcT_i, 1, 1)
            tv = P.val("T")
            P.wait("v", "T", tv)
            for hlf in range(2):
                P.op("v", lambda e, hlf=hlf, bias_row=bias_row: e.tensor_tensor(
                    zs[:, hlf * 512:(hlf + 1) * 512], psb[hlf][:, 0:512],
                    bias_bc[:, bias_row, hlf * 512:(hlf + 1) * 512], ALU.add))
            emit_uhg_norm(P, zs[:], out_ap, scr, 1e-8)

        for sl in range(NSL):
            loadw_phase(wkt_h)
            for i in range(16):
                mk_srcT(i, sl)
                uhg_project(srcT[:, :, :], BK, kv_bf[:, i, :])
                P.wait("s", "V")
                P.op("s", lambda e, i=i: e.activation(qs[:], kv_bf[:, i, :], AF.Square))
                P.wait("v", "S")
                P.op("v", lambda e, i=i: e.tensor_reduce(
                    kn_all[:, i, :].rearrange("p h -> p h ()"),
                    qs[:].rearrange("p (h d) -> p h d", h=16), AX.X, ALU.add))
            loadw_phase(wqt_h)
            for i in range(16):
                mk_srcT(i, sl)
                uhg_project(srcT[:, :, :], BQ, qs[:])
                P.wait("s", "V")
                P.op("s", lambda e: e.activation(zs[:], qs[:], AF.Square))
                P.wait("v", "S")
                P.op("v", lambda e, i=i: e.tensor_reduce(
                    qn_all[:, i, :].rearrange("p h -> p h ()"),
                    zs[:].rearrange("p (h d) -> p h d", h=16), AX.X, ALU.add))
                for j in range(16):
                    P.op("v", lambda e, j=j: e.tensor_tensor(zs[:], qs[:], kv_bf[:, j, :], ALU.mult))
                    P.op("v", lambda e, i=i, j=j: e.tensor_reduce(
                        bass.AP(dots, i * 256 + j, [[4096, 128], [16, 16], [1, 1]]),
                        zs[:].rearrange("p (h d) -> p h d", h=16), AX.X, ALU.add))
            loadw_phase(wvt_h)
            for i in range(16):
                mk_srcT(i, sl)
                uhg_project(srcT[:, :, :], BV, kv_bf[:, i, :])
            for i in range(16):
                P.op("v", lambda e, i=i: e.tensor_tensor(
                    sm1[:], bass.AP(qn_all, i * 16, [[256, 128], [1, 16], [0, 16]]),
                    bass.AP(kn_all, 0, [[256, 128], [1, 16], [16, 16]]), ALU.mult))
                P.op("v", lambda e: e.tensor_scalar(sm1[:], sm1[:], 1e-8, None, ALU.add))
                P.op("v", lambda e: e.reciprocal(sm1[:], sm1[:]))
                P.op("v", lambda e, i=i: e.tensor_tensor(sm2[:], dots[:, i, :, :], dots[:, i, :, :], ALU.mult))
                P.op("v", lambda e: e.tensor_tensor(sm2[:], sm2[:], sm1[:], ALU.mult))
                P.op("v", lambda e: e.tensor_scalar(sm2[:], sm2[:], -1.0, 1.0, ALU.mult, ALU.add))
                P.wait("s", "V")
                P.op("s", lambda e: e.activation(sm2[:], sm2[:], AF.Ln, bias=env["cst"][:, 2:3]), inc="S")
                P.op("s", lambda e: e.activation(sm2[:], sm2[:], AF.Exp, scale=0.5), inc="S")
                P.wait("v", "S")
                P.op("v", lambda e: e.tensor_scalar(sm2[:], sm2[:], -1.0, None, ALU.mult))
                P.op("v", lambda e: e.tensor_reduce(smr[:].rearrange("p h -> p h ()"),
                     sm2[:].rearrange("p (h j) -> p h j", h=16), AX.X, ALU.max))
                P.op("v", lambda e: e.tensor_tensor(sm2[:], sm2[:],
                     bass.AP(smr, 0, [[16, 128], [1, 16], [0, 16]]), ALU.subtract))
                P.wait("s", "V")
                P.op("s", lambda e: e.activation(sm2[:], sm2[:], AF.Exp), inc="S")
                P.wait("v", "S")
                P.op("v", lambda e: e.tensor_reduce(smr[:].rearrange("p h -> p h ()"),
                     sm2[:].rearrange("p (h j) -> p h j", h=16), AX.X, ALU.add))
                P.op("v", lambda e: e.reciprocal(smr[:], smr[:]))
                P.op("v", lambda e, i=i: e.tensor_tensor(attn[:, i, :, :], sm2[:].rearrange("p (h j) -> p h j", h=16),
                     bass.AP(smr, 0, [[16, 128], [1, 16], [0, 16]]), ALU.mult))
            loadw_phase(wot_h)
            for i in range(16):
                P.wait("v", "G")
                P.op("v", lambda e: e.memset(acc[:], 0.0))
                for j in range(16):
                    P.op("v", lambda e, i=i, j=j: e.tensor_tensor(
                        zs[:], kv_bf[:, j, :],
                        bass.AP(attn, i * 256 + j, [[4096, 128], [16, 16], [0, 64]]), ALU.mult))
                    P.op("v", lambda e: e.tensor_tensor(acc[:], acc[:], zs[:], ALU.add))
                P.wait("t", "V")
                for ic in range(8):
                    P.wait("t", "S")
                    P.op("t", lambda e, ic=ic: e.transpose(
                        ps_t[:, (ic % 2) * 128:(ic % 2) * 128 + 128],
                        acc[:, ic * 128:(ic + 1) * 128], env["ident"][:]), inc="T")
                    P.wait("s", "T")
                    P.op("s", lambda e, ic=ic: e.copy(aT[:, ic, :],
                         ps_t[:, (ic % 2) * 128:(ic % 2) * 128 + 128]), inc="S")
                sv = P.val("S")
                P.wait("t", "S", sv)
                P.wait("t", "V")
                proj_tm(aT, 0, 0)
                proj_tm(aT, 1, 1)
                tv = P.val("T")
                P.wait("v", "T", tv)
                for hlf in range(2):
                    P.op("v", lambda e, hlf=hlf: e.tensor_tensor(
                        zs[:, hlf * 512:(hlf + 1) * 512], psb[hlf][:, 0:512],
                        bias_bc[:, BO, hlf * 512:(hlf + 1) * 512], ALU.add))
                emit_uhg_norm(P, zs[:], zs[:], scr, 1e-8)
                ln_src(i, sl, env["ln1"][:])
                P.op("v", lambda e: e.tensor_tensor(zs[:], zs[:], env["ln1"][:], ALU.add))
                emit_uhg_ln(P, zs[:], zs[:], scr, bias_bc[:, G1, :], bias_bc[:, BE1, :])
                P.op("v", lambda e, i=i: e.tensor_copy(x1_all[:, i, :], zs[:]))
            for i in range(16):
                P.wait("t", "V")
                for ic in range(8):
                    P.wait("t", "S")
                    P.op("t", lambda e, i=i, ic=ic: e.transpose(
                        ps_xb[:, (ic % 2) * 128:(ic % 2) * 128 + 128],
                        x1_all[:, i, ic * 128:(ic + 1) * 128], ident_b[:]), inc="T")
                    P.wait("s", "T")
                    P.op("s", lambda e, ic=ic: e.copy(
                        x1T[:, ic, 0:128], ps_xb[:, (ic % 2) * 128:(ic % 2) * 128 + 128]), inc="S")
                sv = P.val("S")
                P.wait("t", "S", sv)
                def rhs1(ic, th):
                    return x1T[:, ic, 0:128]
                def evacw1(oc, th, psl):
                    P.wait("v", "S")
                    P.op("v", lambda e, oc=oc: e.tensor_scalar(sm1[:, 0:128], psl[0][:],
                         b1c[:, oc:oc + 1], None, ALU.add))
                    P.wait("s", "V")
                    P.op("s", lambda e, oc=oc: e.activation(hbuf[:, oc, :], sm1[:, 0:128], AF.Relu), inc="S")
                    P.op("s", lambda e, oc=oc: e.activation(hsq[:, oc, :], sm1[:, 0:128], AF.Square), inc="S")
                emit_fm_stage(P, env, 0, 128, rhs1, [(wA, lambda oc, k0, k1: (w1p_h[oc][:, k0:k1], None))],
                              32, 1024, evacw1, sub_w=1024)
                P.wait("t", "S")
                P.wait("t", "V")
                for oc in range(32):
                    P.op("t", lambda e, oc=oc: e.matmul(psb[1][:, 0:128], onesb[:], hsq[:, oc, :],
                         start=(oc == 0), stop=(oc == 31)), inc="T")
                P.wait("s", "T")
                P.op("s", lambda e: e.activation(sm2[:, 0:128], psb[1][:, 0:128], AF.Ln), inc="S")
                P.op("s", lambda e: e.activation(sm2[:, 0:128], sm2[:, 0:128], AF.Exp, scale=0.5), inc="S")
                P.wait("v", "S")
                P.op("v", lambda e: e.tensor_scalar(sm2[:, 0:128], sm2[:, 0:128], 1e-8, None, ALU.add))
                P.op("v", lambda e: e.reciprocal(sm2[:, 0:128], sm2[:, 0:128]))
                for oc in range(32):
                    P.op("v", lambda e, oc=oc: e.tensor_tensor(hbuf[:, oc, :], hbuf[:, oc, :],
                         sm2[:, 0:128], ALU.mult))
                P.wait("t", "S")
                P.wait("t", "V")
                def rhs2(ic, th):
                    return hbuf[:, ic, 0:128]
                def evacw2(oc, th, psl):
                    P.op("v", lambda e, oc=oc: e.tensor_copy(ffm[:, oc, 0:128], psl[0][:]))
                emit_fm_stage(P, env, 0, 128, rhs2, [(wA, lambda oc, k0, k1: (w2p_h[oc][:, k0:k1], None))],
                              8, 4096, evacw2, sub_w=1024)
                P.wait("t", "V")
                for ic in range(8):
                    P.wait("t", "S")
                    P.wait("t", "V")
                    P.op("t", lambda e, ic=ic: e.transpose(
                        ps_xb[:, (ic % 2) * 128:(ic % 2) * 128 + 128],
                        ffm[:, ic, 0:128], ident_b[:]), inc="T")
                    P.wait("v", "T")
                    P.op("v", lambda e, ic=ic: e.tensor_copy(
                        zs[:, ic * 128:(ic + 1) * 128],
                        ps_xb[:, (ic % 2) * 128:(ic % 2) * 128 + 128]))
                P.op("v", lambda e: e.tensor_tensor(zs[:], zs[:], bias_bc[:, B2, :], ALU.add))
                emit_uhg_norm(P, zs[:], zs[:], scr, 1e-8)
                P.op("v", lambda e, i=i: e.tensor_tensor(zs[:], zs[:], x1_all[:, i, :], ALU.add))
                emit_uhg_ln(P, zs[:], zs[:], scr, bias_bc[:, G2, :], bias_bc[:, BE2, :])
                emit_uhg_norm(P, zs[:], zs[:], scr, 0.0)
                emit_ln(P, zs[:], zs[:], scr)
                P.wait("y", "V")
                P.wait("y", "DM")
                row = i * BC + sl * 128
                P.op("y", lambda e, row=row: e.dma_start(out=env["xq2"][:], in_=x_h[row:row + 128, :]), inc="DM", n=16)
                P.wait("v", "DM")
                P.wait("v", "DO")
                P.op("v", lambda e: e.tensor_tensor(env["yq"][:], zs[:], env["xq2"][:], ALU.add))
                P.wait("g", "V")
                P.op("g", lambda e, row=row: e.dma_start(out=y_h[row:row + 128, :], in_=env["yq"][:]), inc="DO", n=16)
        P.wait("g", "DO")
        with nc.allow_low_precision(reason="bf16 dots/attn tiles; DVE accumulates fp32 internally"):
            P.build(block)
    return nc


def l1attn_launch(x_full, w):
    S, NT, _ = x_full.shape
    BC = NT // NCORES
    nc = build_l1attn(BC)
    import ml_dtypes
    brows = np.stack([np.asarray(w[k], np.float32) for k in
                      ("bq", "bk", "bv", "bo", "b2", "g1", "beta1", "g2", "beta2")])
    shared = {
        "ident": _IDENT, "identb": _IDENT.astype(ml_dtypes.bfloat16),
        "ones1": np.ones((1, 128), np.float32),
        "onesb": np.ones((128, 128), ml_dtypes.bfloat16),
        "wqt": np.ascontiguousarray(np.asarray(w["wq"], np.float32).T).reshape(8, 128, 1024),
        "wkt": np.ascontiguousarray(np.asarray(w["wk"], np.float32).T).reshape(8, 128, 1024),
        "wvt": np.ascontiguousarray(np.asarray(w["wv"], np.float32).T).reshape(8, 128, 1024),
        "wot": np.ascontiguousarray(np.asarray(w["wo"], np.float32).T).reshape(8, 128, 1024),
        "w1p": _sb_panels(np.ascontiguousarray(np.asarray(w["w1"], np.float32).T)),
        "w2p": _sb_panels(np.ascontiguousarray(np.asarray(w["w2"], np.float32).T)),
        "brows": brows, "b1c": _scale_cols(np.asarray(w["b1"], np.float32)),
    }
    in_maps = [dict(shared, x=np.ascontiguousarray(x_full[:, i * BC:(i + 1) * BC, :].reshape(S * BC, D)))
               for i in range(NCORES)]
    res = _run(nc, in_maps)
    return np.concatenate([r["y"].reshape(S, BC, D) for r in res.results], axis=1)


_IDENT = np.ascontiguousarray(np.eye(128, dtype=np.float32))


def _run(nc, in_maps):
    return run_bass_kernel_spmd(nc, in_maps, core_ids=list(range(NCORES)))


def l0ffn_launch(x_full, w):
    TOK = x_full.shape[0] // NCORES
    nc = build_l0ffn(TOK)
    scl = np.zeros((128, 144), np.float32)
    scl[:, 0:32] = _scale_cols(w["w1_s"])
    scl[:, 32:64] = _scale_cols(w["w2_s"])
    scl[:, 64:96] = _scale_cols(w["h1_s"])
    scl[:, 96:128] = _scale_cols(w["h2_s"])
    scl[:, 128:136] = _scale_cols(w["w3_s"])
    shared = {
        "ident": _IDENT, "scl": scl,
        "w1p": _sb_panels(np.ascontiguousarray(w["w1_w"].T)),
        "w2p": _sb_panels(np.ascontiguousarray(w["w2_w"].T)),
        "h1p": _sb_panels(np.ascontiguousarray(w["h1_w"].T)),
        "h2p": _sb_panels(np.ascontiguousarray(w["h2_w"].T)),
        "w3p": _sb_panels(np.ascontiguousarray(w["w3_w"].T)),
    }
    in_maps = [dict(shared, x=np.ascontiguousarray(x_full[i * TOK:(i + 1) * TOK]))
               for i in range(NCORES)]
    res = _run(nc, in_maps)
    return np.concatenate([r["y"] for r in res.results], axis=0)


def kernel(**inputs):
    x = np.asarray(inputs["L0_attn_x"], np.float32)
    B, N, _ = x.shape
    w0a = {k[len("L0_attn_"):]: np.asarray(v) for k, v in inputs.items() if k.startswith("L0_attn_") and k != "L0_attn_x"}
    x1 = l0attn_launch(x, w0a)
    w0f = {k[len("L0_ffn_"):]: np.asarray(v) for k, v in inputs.items() if k.startswith("L0_ffn_")}
    x2 = l0ffn_launch(np.ascontiguousarray(x1.reshape(B * N, D)), w0f).reshape(B, N, D)
    w1a = {k[len("L1_attn_"):]: np.asarray(v) for k, v in inputs.items() if k.startswith("L1_attn_")}
    x3 = l1attn_launch(x2, w1a)
    w1f = {k[len("L1_ffn_"):]: np.asarray(v) for k, v in inputs.items() if k.startswith("L1_ffn_")}
    out = l1ffn_launch(np.ascontiguousarray(x3.reshape(B * N, D)), w1f).reshape(B, N, D)
    return out
